# version 07
# baseline (speedup 1.0000x reference)
import numpy as np
import concourse.bass as bass
import concourse.mybir as mybir
from concourse.bass_utils import run_bass_kernel_spmd

F32 = mybir.dt.float32
BF16 = mybir.dt.bfloat16
I32 = mybir.dt.int32
AF = mybir.ActivationFunctionType
ALU = mybir.AluOpType
AX = mybir.AxisListType

D = 1024
KT = 8
H = 3072
HT = 24
NT = 8
TGS = 512
CH = 128

KB = 1024


def build_nc(OWN=2048, W=512, debug=False):
    NCH = (W + OWN) // CH
    WCH = W // CH
    NTG = OWN // TGS
    BUILD_START = 1

    nc = bass.Bass("TRN2", target_bir_lowering=False)

    xw = nc.dram_tensor("xw", [W + OWN, D], F32, kind="ExternalInput").ap()
    xT = nc.dram_tensor("xT", [D, OWN], F32, kind="ExternalInput").ap()
    conv = nc.dram_tensor("conv", [D, D], F32, kind="ExternalInput").ap()
    corrt = nc.dram_tensor("corrt", [D, D], F32, kind="ExternalInput").ap()
    wgT = nc.dram_tensor("wgT", [D, H], I32, kind="ExternalInput").ap()
    wuT = nc.dram_tensor("wuT", [D, H], I32, kind="ExternalInput").ap()
    wdT = nc.dram_tensor("wdT", [H, D], I32, kind="ExternalInput").ap()
    gcol_d = nc.dram_tensor("gcol", [128, 1], F32, kind="ExternalInput").ap()
    dcol_d = nc.dram_tensor("dcol", [128, 1], F32, kind="ExternalInput").ap()
    identf_d = nc.dram_tensor("identf", [128, 128], F32, kind="ExternalInput").ap()

    outT = nc.dram_tensor("outT", [D, OWN], F32, kind="ExternalOutput").ap()
    if debug:
        dbg_x = nc.dram_tensor("dbg_x", [128, D], F32, kind="ExternalOutput").ap()
        dbg_xn = nc.dram_tensor("dbg_xn", [128, D], F32, kind="ExternalOutput").ap()

    _scrk = "ExternalOutput" if debug else "Internal"
    xnh_scr = nc.dram_tensor("xnh_scr", [D, OWN], BF16, kind=_scrk).ap()
    xnl_scr = nc.dram_tensor("xnl_scr", [D, OWN], BF16, kind=_scrk).ap()
    hoh_scr = nc.dram_tensor("hoh_scr", [D, OWN], BF16, kind="Internal").ap()
    hol_scr = nc.dram_tensor("hol_scr", [D, OWN], BF16, kind="Internal").ap()
    s_scr = nc.dram_tensor("s_scr", [D, OWN], F32, kind=_scrk).ap()
    hh_scr = nc.dram_tensor("hh_scr", [H, OWN], BF16, kind=_scrk).ap()
    hl_scr = nc.dram_tensor("hl_scr", [H, OWN], BF16, kind=_scrk).ap()

    hoh_v = hoh_scr.rearrange("(k p) t -> p k t", p=128)
    hol_v = hol_scr.rearrange("(k p) t -> p k t", p=128)
    xnh_v = xnh_scr.rearrange("(k p) t -> p k t", p=128)
    xnl_v = xnl_scr.rearrange("(k p) t -> p k t", p=128)
    hh_v = hh_scr.rearrange("(k p) t -> p k t", p=128)
    hl_v = hl_scr.rearrange("(k p) t -> p k t", p=128)

    SBASE = 16512
    z1sz = max(32 * KB + 2 * (KT * OWN * 2), 92 * KB)
    z2 = z1sz
    z0 = z1sz + 96 * KB

    def at(name, shape, dtype, off):
        return nc.alloc_sbuf_tensor_at(name, shape, dtype, offset=SBASE + off).ap()

    mth = at("mth", [128, KT, D], BF16, 0)
    mtl = at("mtl", [128, KT, D], BF16, 16 * KB)
    wdb = at("wdb", [128, HT, D], BF16, 0)
    xbuf = [at(f"xbuf{i}", [128, D], F32, 32 * KB + i * 4 * KB) for i in range(6)]
    xnbuf = [at(f"xnbuf{i}", [128, D], F32, 56 * KB + i * 4 * KB) for i in range(3)]
    sqjunk = at("sqjunk", [128, D], F32, 68 * KB)
    stage_hoh = [at(f"stg_hoh{i}", [128, KT * CH], BF16, 72 * KB + i * 2 * KB)
                 for i in range(2)]
    stage_hol = [at(f"stg_hol{i}", [128, KT * CH], BF16, 76 * KB + i * 2 * KB)
                 for i in range(2)]
    ho2h = [at(f"ho2h{i}", [128, KT, TGS], BF16, 32 * KB + i * 8 * KB)
            for i in range(2)]
    ho2l = [at(f"ho2l{i}", [128, KT, TGS], BF16, 48 * KB + i * 8 * KB)
            for i in range(2)]
    xn2h = [at(f"xn2h{i}", [128, 512], BF16, 64 * KB + i * KB) for i in range(2)]
    xn2l = [at(f"xn2l{i}", [128, 512], BF16, 66 * KB + i * KB) for i in range(2)]
    xn3h = [at(f"xn3h{i}", [128, KT, TGS], BF16, 48 * KB + i * KT * TGS * 2)
            for i in range(2)]
    xn3l = [at(f"xn3l{i}", [128, KT, TGS], BF16, 64 * KB + i * KT * TGS * 2)
            for i in range(2)]
    sg_st = [at(f"sg_st{i}", [128, 512], F32, 80 * KB + i * 2 * KB) for i in range(2)]
    h32_st = [at(f"h32_st{i}", [128, 512], F32, 84 * KB + i * 2 * KB) for i in range(2)]
    hh_st = [at(f"hh_st{i}", [128, 512], BF16, 88 * KB + i * KB) for i in range(2)]
    hl_st = [at(f"hl_st{i}", [128, 512], BF16, 90 * KB + i * KB) for i in range(2)]
    g_st = [at(f"g_st{i}", [128, 512], F32, 48 * KB + i * 2 * KB) for i in range(2)]
    out_st = [at(f"out_st{i}", [128, 512], F32, 52 * KB + i * 2 * KB) for i in range(2)]
    s_in = [at(f"s_in{i}", [128, 512], F32, 56 * KB + i * 2 * KB) for i in range(2)]
    x_in = [at(f"x_in{i}", [128, 512], F32, 60 * KB + i * 2 * KB) for i in range(2)]

    convb = at("convb", [128, KT, D], F32, z2)
    corrtb = at("corrtb", [128, KT, D], F32, z2 + 32 * KB)
    holo32 = [at(f"holo32_{i}", [128, KT * CH], F32, 80 * KB + i * 4 * KB)
              for i in range(2)]
    wgb = at("wgb", [128, KT, H], BF16, z2)
    wub = at("wub", [128, KT, H], BF16, z2 + 48 * KB)
    hh3 = [at(f"hh3_{i}", [128, HT, TGS], BF16, z2 + i * 24 * KB) for i in range(2)]
    hl3 = [at(f"hl3_{i}", [128, HT, TGS], BF16, z2 + 48 * KB + i * 24 * KB)
           for i in range(2)]

    identf = at("identf", [128, 128], F32, z0)
    o = z0 + 512
    cols = {}
    _ring_cols = [p + s for p in ["s1", "s2", "m", "ve", "sq", "rs", "nm"]
                  for s in "abcd"]
    for cn in (["gcol", "dcolin", "dcol", "ecol", "m2", "ta", "t1", "t2",
                "r0", "gap1", "gap2"] + _ring_cols):
        cols[cn] = at(cn, [128, 1], F32, o)
        o += 32
    o = (o + 31) // 32 * 32
    stage_xnh = [at(f"stg_xnh{i}", [128, KT * CH], BF16, o + i * 2 * KB)
                 for i in range(2)]
    o += 4 * KB
    stage_xnl = [at(f"stg_xnl{i}", [128, KT * CH], BF16, o + i * 2 * KB)
                 for i in range(2)]
    o += 4 * KB
    s_stage = [at(f"s_stage{i}", [128, 512], F32, o + i * 2 * KB) for i in range(2)]
    o += 4 * KB
    assert SBASE + o <= 229376, f"SBUF overflow: {SBASE + o}"

    pp = [nc.alloc_psum_tensor(f"pp{i}", [128, 1024], F32).ap() for i in range(3)]
    pb = nc.alloc_psum_tensor("pb", [128, 1024], F32).ap()
    pt = ([pp[i // 2][:, (i % 2) * 512:(i % 2) * 512 + 512] for i in range(6)]
          + [pb[:, 0:512], pb[:, 512:1024]])

    import contextlib
    es = contextlib.ExitStack()
    sems = {}
    sem_names = (["q_c", "q_bld", "q_wgu", "q_wd", "q_dbg",
                  "e_v", "e_s", "e_t"]
                 + [f"q_x{i}" for i in range(6)]
                 + [f"q_in2_{i}" for i in range(2)]
                 + [f"q_ho2_{i}" for i in range(2)]
                 + [f"q_spx{i}" for i in range(2)]
                 + [f"q_sps{i}" for i in range(2)]
                 + [f"q_sph{i}" for i in range(2)]
                 + [f"q_in3a{i}" for i in range(2)]
                 + [f"q_h3{i}" for i in range(2)]
                 + [f"q_sx{i}" for i in range(2)]
                 + [f"q_out{i}" for i in range(2)])
    for sn in sem_names:
        sems[sn] = es.enter_context(nc.semaphore(sn))
    C = {sn: 0 for sn in sems}

    OPS = {"sync": [], "vector": [], "scalar": [], "tensor": [], "gpsimd": []}

    def emit(engn, fn):
        OPS[engn].append(fn)

    def wait(engn, sn, val):
        if val <= 0:
            return
        emit(engn, lambda e, s=sems[sn], v=val: e.wait_ge(s, v))

    def bump(sn, by=1):
        C[sn] += by
        return C[sn]

    def op(engn, sn, by, fn):
        v = bump(sn, by)
        emit(engn, lambda e, f=fn, s=sems[sn], b=by: f(e).then_inc(s, b))
        return v

    def selfsync(engn, sn):
        wait(engn, sn, C[sn])

    op("sync", "q_c", 16, lambda e: e.dma_start(out=identf, in_=identf_d))
    op("sync", "q_c", 16, lambda e: e.dma_start(out=cols["gcol"], in_=gcol_d))
    op("sync", "q_c", 16, lambda e: e.dma_start(out=cols["dcolin"], in_=dcol_d))
    consts_q = C["q_c"]

    wait("scalar", "q_c", consts_q)
    e_exp = op("scalar", "e_s", 1, lambda e: e.activation(
        out=cols["ecol"], in_=cols["dcolin"], func=AF.Exp, scale=-1.0))
    wait("vector", "e_s", e_exp)
    op("vector", "e_v", 1, lambda e: e.tensor_scalar_add(
        out=cols["ta"], in0=cols["ecol"], scalar1=1.0))
    selfsync("vector", "e_v")
    e_dcol = op("vector", "e_v", 1, lambda e: e.reciprocal(out=cols["dcol"],
                                                           in_=cols["ta"]))
    dcol_bc = cols["dcol"][:, :1].broadcast_to((128, CH))

    def load_x(c):
        return op("sync", f"q_x{c % 6}", 16, lambda e, c=c: e.dma_start(
            out=xbuf[c % 6][:, :], in_=xw[c * CH:(c + 1) * CH, :]))

    x_q = {}
    conv_v = conv.rearrange("(k p) i -> p k i", p=128)
    corrt_v = corrt.rearrange("(k p) j -> p k j", p=128)
    op("sync", "q_bld", 16, lambda e: e.dma_start(out=convb[:, :, :], in_=conv_v))
    op("sync", "q_bld", 16, lambda e: e.dma_start(out=corrtb[:, :, :], in_=corrt_v))
    build_in_q = C["q_bld"]

    norm_done = {}
    sq_done = {}
    vareps_done = {}
    sqrt_done = {}
    stats_done = {}
    tp_done = {}
    scan_done = {}
    xnh_done = {}
    hoh_done = {}
    rel_done = {}
    chunkv_done = {}
    spx_cnt = {}
    bu_mm_done = {}
    bu_mth_done = {}
    bu_split_done = {}

    n_build = 16
    bu_list = [(mi, nj) for nj in range(2) for mi in range(NT)]
    next_bu = 0
    pending_splits = []

    def emit_build_splits(upto=10**9):
        while pending_splits and pending_splits[0][0] <= upto:
            u, mi, nj, bank = pending_splits.pop(0)
            wait("scalar", "e_t", bu_mm_done[u])
            bu_mth_done[u] = op("scalar", "e_s", 1,
                                lambda e, mi=mi, nj=nj, bank=bank: e.activation(
                out=mth[:, mi, nj * 512:(nj + 1) * 512],
                in_=pt[bank][:, 0:512],
                func=AF.Identity, scale=cols["gcol"][:, :1]))
            wait("vector", "e_s", bu_mth_done[u])
            selfsync("vector", "e_v")
            op("vector", "e_v", 1, lambda e, bank=bank: e.tensor_scalar_mul(
                out=s_stage[0][:, 0:512], in0=pt[bank][:, 0:512],
                scalar1=cols["gcol"][:, :1]))
            selfsync("vector", "e_v")
            bu_split_done[u] = op("vector", "e_v", 1,
                                  lambda e, mi=mi, nj=nj: e.tensor_tensor(
                out=mtl[:, mi, nj * 512:(nj + 1) * 512],
                in0=s_stage[0][:, 0:512],
                in1=mth[:, mi, nj * 512:(nj + 1) * 512], op=ALU.subtract))

    for c in range(min(6, NCH)):
        x_q[c] = load_x(c)

    def sfx_of(c):
        return "abcd"[c % 4]

    NSTAGE = 10
    for t in range(NCH + NSTAGE):
        emit_build_splits()
        c = t
        if 0 <= c < NCH:
            sfx = sfx_of(c)
            wait("scalar", f"q_x{c % 6}", x_q[c])
            selfsync("scalar", "e_s")
            sq_done[c] = op("scalar", "e_s", 1,
                            lambda e, c=c, sfx=sfx: e.activation(
                out=sqjunk[:, :], in_=xbuf[c % 6][:, :], func=AF.Square,
                accum_out=cols["s2" + sfx][:, :1]))

        c = t - 1
        if 0 <= c < NCH:
            sfx = sfx_of(c)
            wait("vector", f"q_x{c % 6}", x_q[c])
            op("vector", "e_v", 1, lambda e, c=c, sfx=sfx: e.reduce_sum(
                out=cols["s1" + sfx][:, :1], in_=xbuf[c % 6][:, :], axis=AX.X))
            wait("vector", "e_s", sq_done[c])
            selfsync("vector", "e_v")
            op("vector", "e_v", 1, lambda e, sfx=sfx: e.tensor_scalar_mul(
                out=cols["m" + sfx], in0=cols["s1" + sfx], scalar1=1.0 / D))
            emit("vector", lambda e, sfx=sfx: e.tensor_scalar(
                out=cols["ta"], in0=cols["s2" + sfx], scalar1=1.0 / D,
                scalar2=1e-5, op0=ALU.mult, op1=ALU.add))
            selfsync("vector", "e_v")
            op("vector", "e_v", 1, lambda e, sfx=sfx: e.tensor_scalar(
                out=cols["m2"], in0=cols["m" + sfx],
                scalar1=cols["m" + sfx][:, :1], scalar2=None, op0=ALU.mult))
            selfsync("vector", "e_v")
            vareps_done[c] = op("vector", "e_v", 1,
                                lambda e, sfx=sfx: e.scalar_tensor_tensor(
                out=cols["ve" + sfx], in0=cols["m2"], scalar=-1.0,
                in1=cols["ta"], op0=ALU.mult, op1=ALU.add))

        c = t - 2
        if 0 <= c < NCH:
            sfx = sfx_of(c)
            wait("scalar", "e_v", vareps_done[c])
            sqrt_done[c] = op("scalar", "e_s", 1,
                              lambda e, sfx=sfx: e.activation(
                out=cols["sq" + sfx], in_=cols["ve" + sfx], func=AF.Sqrt))

        c = t - 3
        if 0 <= c < NCH:
            sfx = sfx_of(c)
            wait("vector", "e_s", sqrt_done[c])
            op("vector", "e_v", 1, lambda e, sfx=sfx: e.reciprocal(
                out=cols["r0"], in_=cols["sq" + sfx]))
            selfsync("vector", "e_v")
            op("vector", "e_v", 1, lambda e: e.tensor_scalar(
                out=cols["t1"], in0=cols["r0"], scalar1=cols["r0"][:, :1],
                scalar2=None, op0=ALU.mult))
            selfsync("vector", "e_v")
            op("vector", "e_v", 1, lambda e, sfx=sfx: e.tensor_scalar(
                out=cols["t2"], in0=cols["t1"], scalar1=cols["ve" + sfx][:, :1],
                scalar2=-0.5, op0=ALU.mult, op1=ALU.mult))
            selfsync("vector", "e_v")
            op("vector", "e_v", 1, lambda e, sfx=sfx: e.scalar_tensor_tensor(
                out=cols["rs" + sfx], in0=cols["t2"], scalar=1.5,
                in1=cols["r0"], op0=ALU.add, op1=ALU.mult))
            selfsync("vector", "e_v")
            stats_done[c] = op("vector", "e_v", 1,
                               lambda e, sfx=sfx: e.scalar_tensor_tensor(
                out=cols["nm" + sfx], in0=cols["m" + sfx], scalar=-1.0,
                in1=cols["rs" + sfx], op0=ALU.mult, op1=ALU.mult))

        c = t - 4
        if 0 <= c < NCH:
            sfx = sfx_of(c)
            wait("scalar", "e_v", stats_done[c])
            wait("scalar", "e_t", tp_done.get(c - 3, 0))
            norm_done[c] = op("scalar", "e_s", 1,
                              lambda e, c=c, sfx=sfx: e.activation(
                out=xnbuf[c % 3][:, :], in_=xbuf[c % 6][:, :], func=AF.Identity,
                scale=cols["rs" + sfx][:, :1], bias=cols["nm" + sfx][:, :1]))
            if debug and c == WCH:
                op("scalar", "q_dbg", 16, lambda e, c=c: e.dma_start(
                    out=dbg_x, in_=xbuf[c % 6][:, :]))
                op("scalar", "q_dbg", 16, lambda e, c=c: e.dma_start(
                    out=dbg_xn, in_=xnbuf[c % 3][:, :]))

        c = t + 1
        if 6 <= c < NCH:
            wait("sync", "e_s", norm_done[c - 6])
            x_q[c] = load_x(c)

        c = t - 8
        if 0 <= c < NCH:
            oc = c - WCH
            if oc >= 0:
                wait("vector", "e_s", hoh_done[c])
                wait("vector", "e_v", chunkv_done.get(c - 2, 0))
                emit("vector", lambda e, c=c: e.tensor_tensor(
                    out=stage_xnl[(c - WCH) % 2][:, :], in0=pp[c % 3][:, :],
                    in1=stage_xnh[(c - WCH) % 2][:, :], op=ALU.subtract))
                selfsync("vector", "e_v")
                chunkv_done[c] = op("vector", "e_v", 1,
                                    lambda e, c=c, oc=oc: e.tensor_tensor(
                    out=stage_hol[oc % 2][:, :], in0=holo32[c % 2][:, :],
                    in1=stage_hoh[oc % 2][:, :], op=ALU.subtract))
                rel_done[c] = chunkv_done[c]

        if 0 <= c < NCH:
            oc = c - WCH
            if oc >= 0:
                wait("scalar", "e_v", chunkv_done[c])
                for scr_v, stg in ((xnh_v, stage_xnh), (xnl_v, stage_xnl),
                                   (hoh_v, stage_hoh), (hol_v, stage_hol)):
                    spx_cnt[oc] = op("scalar", f"q_spx{oc % 2}", 16,
                                     lambda e, oc=oc, scr_v=scr_v, stg=stg: e.dma_start(
                        out=scr_v[:, :, oc * CH:(oc + 1) * CH],
                        in_=stg[oc % 2].rearrange("p (k t) -> p k t", k=KT)))

        c = t - 5
        if 0 <= c < NCH:
            wait("tensor", "e_s", norm_done[c])
            wait("tensor", "q_c", consts_q)
            if c - 3 >= 0:
                assert (c - 3) in rel_done, f"rel_done missing for {c - 3}"
                if c - 3 >= WCH:
                    assert (c - 3) in xnh_done
            wait("tensor", "e_s", xnh_done.get(c - 3, 0))
            wait("tensor", "e_v", rel_done.get(c - 3, 0))
            p0, p1 = 2 * (c % 3), 2 * (c % 3) + 1
            for k in range(KT):
                bank, sl = (p0, k) if k < 4 else (p1, k - 4)
                fn = lambda e, c=c, k=k, bank=bank, sl=sl: e.transpose(
                    out=pt[bank][:, sl * 128:(sl + 1) * 128],
                    in_=xnbuf[c % 3][:, k * 128:(k + 1) * 128], identity=identf)
                if k == KT - 1:
                    tp_done[c] = op("tensor", "e_t", 1, fn)
                else:
                    emit("tensor", fn)

            if c >= BUILD_START and next_bu < n_build:
                n_units = (n_build - next_bu) if c == NCH - 1 else 2
                for _ in range(n_units):
                    if next_bu >= n_build:
                        break
                    u = next_bu
                    mi, nj = bu_list[u]
                    emit_build_splits(u - 2)
                    wait("tensor", "q_bld", build_in_q)
                    wait("tensor", "e_v", bu_split_done.get(u - 2, 0))
                    bank = 6 + u % 2
                    for k in range(KT):
                        fn = lambda e, k=k, mi=mi, nj=nj, bank=bank: e.matmul(
                            out=pt[bank][:, 0:512],
                            lhsT=convb[:, k, mi * 128:(mi + 1) * 128],
                            rhs=corrtb[:, k, nj * 512:(nj + 1) * 512],
                            start=(k == 0), stop=(k == KT - 1))
                        if k == KT - 1:
                            bu_mm_done[u] = op("tensor", "e_t", 1, fn)
                        else:
                            emit("tensor", fn)
                    pending_splits.append((u, mi, nj, bank))
                    next_bu += 1

        c = t - 6
        if 0 <= c < NCH:
            wait("vector", "e_t", tp_done[c])
            wait("vector", "e_s", hoh_done.get(c - 2, 0))
            wait("vector", "e_v", scan_done.get(c - 1, 0))
            wait("vector", "e_v", chunkv_done.get(c - 2, 0))
            if c == 0:
                wait("vector", "e_v", e_dcol)
            p0, p1 = 2 * (c % 3), 2 * (c % 3) + 1
            for k in range(KT):
                bank, sl = (p0, k) if k < 4 else (p1, k - 4)
                init = 0.0 if c == 0 else holo32[(c - 1) % 2][:, k * CH + CH - 1:
                                                              k * CH + CH]
                fn = lambda e, c=c, k=k, bank=bank, sl=sl, init=init: (
                    e.tensor_tensor_scan(
                        out=holo32[c % 2][:, k * CH:(k + 1) * CH], data0=dcol_bc,
                        data1=pt[bank][:, sl * 128:(sl + 1) * 128], initial=init,
                        op0=ALU.mult, op1=ALU.add))
                if k == KT - 1:
                    scan_done[c] = op("vector", "e_v", 1, fn)
                else:
                    emit("vector", fn)
            if c < WCH:
                rel_done[c] = scan_done[c]

        c = t - 7
        if 0 <= c < NCH:
            oc = c - WCH
            if oc >= 0:
                wait("scalar", "e_t", tp_done[c])
                wait("scalar", "e_v", scan_done[c])
                wait("scalar", f"q_spx{oc % 2}", spx_cnt.get(oc - 2, 0))
                selfsync("scalar", "e_s")
                xnh_done[c] = op("scalar", "e_s", 1,
                                 lambda e, c=c, oc=oc: e.copy(
                    out=stage_xnh[oc % 2][:, :], in_=pp[c % 3][:, :]))
                hoh_done[c] = op("scalar", "e_s", 1,
                                 lambda e, c=c, oc=oc: e.copy(
                    out=stage_hoh[oc % 2][:, :], in_=holo32[c % 2][:, :]))
            else:
                hoh_done[c] = 0

    emit_build_splits()
    assert next_bu == n_build
    P1_ev, P1_es, P1_et = C["e_v"], C["e_s"], C["e_t"]
    P1_spx = (C["q_spx0"], C["q_spx1"])

    wait("gpsimd", "e_t", bu_mm_done[n_build - 1])
    for k in range(KT):
        op("gpsimd", "q_wgu", 16, lambda e, k=k: e.dma_start(
            out=wgb[:, k, :], in_=wgT[k * 128:(k + 1) * 128, :]))
    wait("gpsimd", "e_v", P1_ev)
    for k in range(KT):
        op("gpsimd", "q_wgu", 16, lambda e, k=k: e.dma_start(
            out=wub[:, k, :], in_=wuT[k * 128:(k + 1) * 128, :]))
    W_gu = C["q_wgu"]

    wait("tensor", "e_v", P1_ev)
    wait("vector", "e_t", P1_et)
    wait("scalar", "e_v", P1_ev)
    wait("sync", "e_v", P1_ev)
    wait("sync", "q_spx0", P1_spx[0])
    wait("sync", "q_spx1", P1_spx[1])

    ap_done = {}
    sev_done = {}
    in2_q = {}
    ho2_q = {}
    sps_cnt = {}
    u = 0
    for tg in range(NTG):
        if tg >= 2:
            wait("sync", "e_t", ap_done.get((tg - 1) * NT - 1, 0))
        op("sync", f"q_ho2_{tg % 2}", 16, lambda e, tg=tg: e.dma_start(
            out=ho2h[tg % 2][:, :, :], in_=hoh_v[:, :, tg * TGS:(tg + 1) * TGS]))
        ho2_q[tg] = op("sync", f"q_ho2_{tg % 2}", 16, lambda e, tg=tg: e.dma_start(
            out=ho2l[tg % 2][:, :, :], in_=hol_v[:, :, tg * TGS:(tg + 1) * TGS]))
        for n in range(NT):
            wait("sync", "e_v", sev_done.get(u - 2, 0))
            op("sync", f"q_in2_{u % 2}", 16, lambda e, u=u, n=n, tg=tg: e.dma_start(
                out=xn2h[u % 2][:, 0:512], in_=xnh_v[:, n, tg * TGS:(tg + 1) * TGS]))
            in2_q[u] = op("sync", f"q_in2_{u % 2}", 16, lambda e, u=u, n=n, tg=tg: e.dma_start(
                out=xn2l[u % 2][:, 0:512], in_=xnl_v[:, n, tg * TGS:(tg + 1) * TGS]))

            bank = u % 4
            wait("tensor", "e_v", sev_done.get(u - 4, 0))
            wait("tensor", f"q_ho2_{tg % 2}", ho2_q[tg])
            nmm = 0
            for (lh, rh) in ((mth, ho2h), (mth, ho2l), (mtl, ho2h)):
                for k in range(KT):
                    nmm += 1
                    fn = lambda e, lh=lh, rh=rh, k=k, n=n, tg=tg, bank=bank, \
                        st=(nmm == 1), sp=(nmm == 24): e.matmul(
                        out=pt[bank][:, 0:512], lhsT=lh[:, k, n * 128:(n + 1) * 128],
                        rhs=rh[tg % 2][:, k, :], start=st, stop=sp)
                    if nmm == 24:
                        ap_done[u] = op("tensor", "e_t", 1, fn)
                    else:
                        emit("tensor", fn)

            wait("vector", "e_t", ap_done[u])
            wait("vector", f"q_in2_{u % 2}", in2_q[u])
            wait("vector", f"q_sps{u % 2}", sps_cnt.get(u - 2, 0))
            wait("vector", "e_v", sev_done.get(u - 2, 0))
            op("vector", "e_v", 1, lambda e, u=u, bank=bank: e.tensor_tensor(
                out=s_stage[u % 2][:, 0:512], in0=pt[bank][:, 0:512],
                in1=xn2h[u % 2][:, 0:512], op=ALU.add))
            selfsync("vector", "e_v")
            sev_done[u] = op("vector", "e_v", 1, lambda e, u=u: e.tensor_tensor(
                out=s_stage[u % 2][:, 0:512], in0=s_stage[u % 2][:, 0:512],
                in1=xn2l[u % 2][:, 0:512], op=ALU.add))

            wait("scalar", "e_v", sev_done[u])
            sps_cnt[u] = op("scalar", f"q_sps{u % 2}", 16,
                            lambda e, u=u, n=n, tg=tg: e.dma_start(
                out=s_scr[n * 128:(n + 1) * 128, tg * TGS:(tg + 1) * TGS],
                in_=s_stage[u % 2][:, 0:512]))
            u += 1

    P2_ev, P2_es, P2_et = C["e_v"], C["e_s"], C["e_t"]
    P2_sps = (C["q_sps0"], C["q_sps1"])

    wait("gpsimd", "e_v", P2_ev)
    wait("gpsimd", "e_t", P2_et)
    for k in range(HT):
        op("gpsimd", "q_wd", 16, lambda e, k=k: e.dma_start(
            out=wdb[:, k, :], in_=wdT[k * 128:(k + 1) * 128, :]))
    W_d = C["q_wd"]

    wait("tensor", "e_v", P2_ev)
    wait("tensor", "q_wgu", W_gu)
    wait("vector", "e_t", P2_et)
    wait("scalar", "e_v", P2_ev)
    wait("sync", "e_v", P2_ev)

    in3a_q = {}
    gate_done = {}
    up_done = {}
    sg_done = {}
    h32_done = {}
    hl_done = {}
    sph_cnt = {}
    u = 0
    for tg in range(NTG):
        if tg >= 2:
            wait("sync", "e_t", up_done.get((tg - 1) * HT - 1, 0))
        op("sync", f"q_in3a{tg % 2}", 16, lambda e, tg=tg: e.dma_start(
            out=xn3h[tg % 2][:, :, :], in_=xnh_v[:, :, tg * TGS:(tg + 1) * TGS]))
        in3a_q[tg] = op("sync", f"q_in3a{tg % 2}", 16, lambda e, tg=tg: e.dma_start(
            out=xn3l[tg % 2][:, :, :], in_=xnl_v[:, :, tg * TGS:(tg + 1) * TGS]))

        for m in range(HT):
            bA = u % 2
            bB = 2 + u % 2
            wait("tensor", f"q_in3a{tg % 2}", in3a_q[tg])
            wait("tensor", "e_s", sg_done.get(u - 2, 0))
            nmm = 0
            for src in (xn3h, xn3l):
                for k in range(KT):
                    nmm += 1
                    fn = lambda e, src=src, k=k, m=m, tg=tg, bA=bA, \
                        st=(nmm == 1), sp=(nmm == 16): e.matmul(
                        out=pt[bA][:, 0:512], lhsT=wgb[:, k, m * 128:(m + 1) * 128],
                        rhs=src[tg % 2][:, k, :], start=st, stop=sp)
                    if nmm == 16:
                        gate_done[u] = op("tensor", "e_t", 1, fn)
                    else:
                        emit("tensor", fn)
            wait("tensor", "e_v", h32_done.get(u - 2, 0))
            nmm = 0
            for src in (xn3h, xn3l):
                for k in range(KT):
                    nmm += 1
                    fn = lambda e, src=src, k=k, m=m, tg=tg, bB=bB, \
                        st=(nmm == 1), sp=(nmm == 16): e.matmul(
                        out=pt[bB][:, 0:512], lhsT=wub[:, k, m * 128:(m + 1) * 128],
                        rhs=src[tg % 2][:, k, :], start=st, stop=sp)
                    if nmm == 16:
                        up_done[u] = op("tensor", "e_t", 1, fn)
                    else:
                        emit("tensor", fn)

            wait("scalar", "e_t", gate_done[u])
            wait("scalar", "e_v", h32_done.get(u - 2, 0))
            sg_done[u] = op("scalar", "e_s", 1, lambda e, u=u, bA=bA: e.activation(
                out=sg_st[u % 2][:, 0:512], in_=pt[bA][:, 0:512], func=AF.Silu,
                scale=1.0 / 127.0))
            wait("vector", "e_t", up_done[u])
            wait("vector", "e_s", sg_done[u])
            h32_done[u] = op("vector", "e_v", 1, lambda e, u=u, bB=bB: e.scalar_tensor_tensor(
                out=h32_st[u % 2][:, 0:512], in0=pt[bB][:, 0:512], scalar=1.0 / 127.0,
                in1=sg_st[u % 2][:, 0:512], op0=ALU.mult, op1=ALU.mult))
            wait("scalar", "e_v", h32_done[u])
            wait("scalar", f"q_sph{u % 2}", sph_cnt.get(u - 2, 0))
            hh_done = op("scalar", "e_s", 1, lambda e, u=u: e.copy(
                out=hh_st[u % 2][:, 0:512], in_=h32_st[u % 2][:, 0:512]))
            wait("vector", "e_s", hh_done)
            hl_done[u] = op("vector", "e_v", 1, lambda e, u=u: e.tensor_tensor(
                out=hl_st[u % 2][:, 0:512], in0=h32_st[u % 2][:, 0:512],
                in1=hh_st[u % 2][:, 0:512], op=ALU.subtract))
            wait("scalar", "e_v", hl_done[u])
            op("scalar", f"q_sph{u % 2}", 16, lambda e, u=u, m=m, tg=tg: e.dma_start(
                out=hh_scr[m * 128:(m + 1) * 128, tg * TGS:(tg + 1) * TGS],
                in_=hh_st[u % 2][:, 0:512]))
            sph_cnt[u] = op("scalar", f"q_sph{u % 2}", 16,
                            lambda e, u=u, m=m, tg=tg: e.dma_start(
                out=hl_scr[m * 128:(m + 1) * 128, tg * TGS:(tg + 1) * TGS],
                in_=hl_st[u % 2][:, 0:512]))
            u += 1

    P3a_ev, P3a_es, P3a_et = C["e_v"], C["e_s"], C["e_t"]
    P3a_sph = (C["q_sph0"], C["q_sph1"])

    wait("tensor", "e_v", P3a_ev)
    wait("tensor", "q_wd", W_d)
    wait("vector", "e_t", P3a_et)
    wait("scalar", "e_v", P3a_ev)
    wait("sync", "e_t", P3a_et)
    wait("gpsimd", "e_t", P3a_et)
    wait("gpsimd", "e_v", P3a_ev)
    wait("sync", "q_sph0", P3a_sph[0])
    wait("sync", "q_sph1", P3a_sph[1])
    wait("sync", "q_sps0", P2_sps[0])
    wait("sync", "q_sps1", P2_sps[1])

    in3b_q = {}
    sx_q = {}
    dn_done = {}
    g_done = {}
    out_done = {}
    out_cnt = {}
    u = 0
    for tg in range(NTG):
        if tg >= 2:
            wait("sync", "e_t", dn_done.get((tg - 1) * NT - 1, 0))
        op("sync", f"q_h3{tg % 2}", 16, lambda e, tg=tg: e.dma_start(
            out=hh3[tg % 2][:, :, :], in_=hh_v[:, :, tg * TGS:(tg + 1) * TGS]))
        in3b_q[tg] = op("sync", f"q_h3{tg % 2}", 16, lambda e, tg=tg: e.dma_start(
            out=hl3[tg % 2][:, :, :], in_=hl_v[:, :, tg * TGS:(tg + 1) * TGS]))

        for n in range(NT):
            bank = 4 + u % 4
            wait("gpsimd", "e_v", out_done.get(u - 2, 0))
            op("gpsimd", f"q_sx{u % 2}", 16, lambda e, u=u, n=n, tg=tg: e.dma_start(
                out=s_in[u % 2][:, 0:512],
                in_=s_scr[n * 128:(n + 1) * 128, tg * TGS:(tg + 1) * TGS]))
            sx_q[u] = op("gpsimd", f"q_sx{u % 2}", 16, lambda e, u=u, n=n, tg=tg: e.dma_start(
                out=x_in[u % 2][:, 0:512],
                in_=xT[n * 128:(n + 1) * 128, tg * TGS:(tg + 1) * TGS]))

            wait("tensor", f"q_h3{tg % 2}", in3b_q[tg])
            wait("tensor", "e_s", g_done.get(u - 4, 0))
            nmm = 0
            for src in (hh3, hl3):
                for k in range(HT):
                    nmm += 1
                    fn = lambda e, src=src, k=k, n=n, tg=tg, bank=bank, \
                        st=(nmm == 1), sp=(nmm == 48): e.matmul(
                        out=pt[bank][:, 0:512], lhsT=wdb[:, k, n * 128:(n + 1) * 128],
                        rhs=src[tg % 2][:, k, :], start=st, stop=sp)
                    if nmm == 48:
                        dn_done[u] = op("tensor", "e_t", 1, fn)
                    else:
                        emit("tensor", fn)

            wait("scalar", "e_t", dn_done[u])
            wait("scalar", "e_v", out_done.get(u - 2, 0))
            g_done[u] = op("scalar", "e_s", 1, lambda e, u=u, bank=bank: e.activation(
                out=g_st[u % 2][:, 0:512], in_=pt[bank][:, 0:512], func=AF.Silu,
                scale=1.0 / 127.0))
            wait("vector", "e_s", g_done[u])
            wait("vector", f"q_sx{u % 2}", sx_q[u])
            wait("vector", f"q_out{u % 2}", out_cnt.get(u - 2, 0))
            wait("vector", "e_v", out_done.get(u - 2, 0))
            op("vector", "e_v", 1, lambda e, u=u: e.tensor_tensor(
                out=out_st[u % 2][:, 0:512], in0=g_st[u % 2][:, 0:512],
                in1=s_in[u % 2][:, 0:512], op=ALU.mult))
            selfsync("vector", "e_v")
            out_done[u] = op("vector", "e_v", 1, lambda e, u=u: e.tensor_tensor(
                out=out_st[u % 2][:, 0:512], in0=out_st[u % 2][:, 0:512],
                in1=x_in[u % 2][:, 0:512], op=ALU.add))
            wait("scalar", "e_v", out_done[u])
            out_cnt[u] = op("scalar", f"q_out{u % 2}", 16,
                            lambda e, u=u, n=n, tg=tg: e.dma_start(
                out=outT[n * 128:(n + 1) * 128, tg * TGS:(tg + 1) * TGS],
                in_=out_st[u % 2][:, 0:512]))
            u += 1

    wait("scalar", "q_out0", C["q_out0"])
    wait("scalar", "q_out1", C["q_out1"])

    with nc.Block() as block:
        @block.sync
        def _(eng):
            for fn in OPS["sync"]:
                fn(eng)

        @block.vector
        def _(eng):
            for fn in OPS["vector"]:
                fn(eng)

        @block.scalar
        def _(eng):
            for fn in OPS["scalar"]:
                fn(eng)

        @block.tensor
        def _(eng):
            for fn in OPS["tensor"]:
                fn(eng)

        @block.gpsimd
        def _(eng):
            for fn in OPS["gpsimd"]:
                fn(eng)

    es.close()
    return nc


def kernel(L0_attn_x, L0_attn_bind_key, L0_attn_unbind_key, L0_attn_gate,
           L0_attn_decay, L0_ffn_w_gate, L0_ffn_w_up, L0_ffn_w_down):
    OWN, W_ = 2048, 512
    x = np.ascontiguousarray(np.asarray(L0_attn_x, dtype=np.float32))
    kb = np.asarray(L0_attn_bind_key, dtype=np.float32)
    ku = np.asarray(L0_attn_unbind_key, dtype=np.float32)
    gate = np.asarray(L0_attn_gate, dtype=np.float32).reshape(-1)[0]
    decay = np.asarray(L0_attn_decay, dtype=np.float32).reshape(())

    k_ = np.arange(D)[:, None]
    i_ = np.arange(D)[None, :]
    conv = np.ascontiguousarray(kb[(k_ - i_) % D])
    corrt = np.ascontiguousarray(ku[(k_ - i_) % D])
    wgT = np.ascontiguousarray(np.asarray(L0_ffn_w_gate, dtype=np.int32).T)
    wuT = np.ascontiguousarray(np.asarray(L0_ffn_w_up, dtype=np.int32).T)
    wdT = np.ascontiguousarray(np.asarray(L0_ffn_w_down, dtype=np.int32).T)
    gcol = np.full((128, 1), gate, np.float32)
    dcol = np.full((128, 1), decay, np.float32)
    identf = np.eye(128, dtype=np.float32)

    nc = build_nc(OWN, W_)
    in_maps = []
    for c in range(8):
        b, q = c // 4, c % 4
        t0 = q * OWN
        xw = np.zeros((W_ + OWN, D), np.float32)
        if q > 0:
            xw[:W_] = x[b, t0 - W_:t0]
        xw[W_:] = x[b, t0:t0 + OWN]
        in_maps.append(dict(
            xw=xw, xT=np.ascontiguousarray(x[b, t0:t0 + OWN].T),
            conv=conv, corrt=corrt, wgT=wgT, wuT=wuT, wdT=wdT,
            gcol=gcol, dcol=dcol, identf=identf))
    res = run_bass_kernel_spmd(nc, in_maps, core_ids=list(range(8)))
    out = np.empty_like(x)
    for c in range(8):
        b, q = c // 4, c % 4
        out[b, q * OWN:(q + 1) * OWN] = res.results[c]["outT"].T
    return out


# version 08
# speedup vs baseline: 1.0276x; 1.0276x over previous
import numpy as np
import concourse.bass as bass
import concourse.mybir as mybir
from concourse.bass_utils import run_bass_kernel_spmd

F32 = mybir.dt.float32
BF16 = mybir.dt.bfloat16
I32 = mybir.dt.int32
AF = mybir.ActivationFunctionType
ALU = mybir.AluOpType
AX = mybir.AxisListType

D = 1024
KT = 8
H = 3072
HT = 24
NT = 8
TGS = 512
CH = 128

KB = 1024


def build_nc(OWN=2048, W=512, debug=False):
    NCH = (W + OWN) // CH
    WCH = W // CH
    NTG = OWN // TGS
    BUILD_START = 1

    nc = bass.Bass("TRN2", target_bir_lowering=False)

    xw = nc.dram_tensor("xw", [W + OWN, D], F32, kind="ExternalInput").ap()
    xT = nc.dram_tensor("xT", [D, OWN], F32, kind="ExternalInput").ap()
    conv = nc.dram_tensor("conv", [D, D], F32, kind="ExternalInput").ap()
    corrt = nc.dram_tensor("corrt", [D, D], F32, kind="ExternalInput").ap()
    wgT = nc.dram_tensor("wgT", [D, H], I32, kind="ExternalInput").ap()
    wuT = nc.dram_tensor("wuT", [D, H], I32, kind="ExternalInput").ap()
    wdT = nc.dram_tensor("wdT", [H, D], I32, kind="ExternalInput").ap()
    gcol_d = nc.dram_tensor("gcol", [128, 1], F32, kind="ExternalInput").ap()
    dcol_d = nc.dram_tensor("dcol", [128, 1], F32, kind="ExternalInput").ap()
    identf_d = nc.dram_tensor("identf", [128, 128], F32, kind="ExternalInput").ap()

    outT = nc.dram_tensor("outT", [D, OWN], F32, kind="ExternalOutput").ap()
    if debug:
        dbg_x = nc.dram_tensor("dbg_x", [128, D], F32, kind="ExternalOutput").ap()
        dbg_xn = nc.dram_tensor("dbg_xn", [128, D], F32, kind="ExternalOutput").ap()

    _scrk = "ExternalOutput" if debug else "Internal"
    xnh_scr = nc.dram_tensor("xnh_scr", [D, OWN], BF16, kind=_scrk).ap()
    xnl_scr = nc.dram_tensor("xnl_scr", [D, OWN], BF16, kind=_scrk).ap()
    hoh_scr = nc.dram_tensor("hoh_scr", [D, OWN], BF16, kind="Internal").ap()
    hol_scr = nc.dram_tensor("hol_scr", [D, OWN], BF16, kind="Internal").ap()
    s_scr = nc.dram_tensor("s_scr", [D, OWN], F32, kind=_scrk).ap()
    hh_scr = nc.dram_tensor("hh_scr", [H, OWN], BF16, kind=_scrk).ap()
    hl_scr = nc.dram_tensor("hl_scr", [H, OWN], BF16, kind=_scrk).ap()

    hoh_v = hoh_scr.rearrange("(k p) t -> p k t", p=128)
    hol_v = hol_scr.rearrange("(k p) t -> p k t", p=128)
    xnh_v = xnh_scr.rearrange("(k p) t -> p k t", p=128)
    xnl_v = xnl_scr.rearrange("(k p) t -> p k t", p=128)
    hh_v = hh_scr.rearrange("(k p) t -> p k t", p=128)
    hl_v = hl_scr.rearrange("(k p) t -> p k t", p=128)

    SBASE = 16512
    z1sz = max(32 * KB + 2 * (KT * OWN * 2), 92 * KB)
    z2 = z1sz
    z0 = z1sz + 96 * KB

    def at(name, shape, dtype, off):
        return nc.alloc_sbuf_tensor_at(name, shape, dtype, offset=SBASE + off).ap()

    mth = at("mth", [128, KT, D], BF16, 0)
    mtl = at("mtl", [128, KT, D], BF16, 16 * KB)
    wdb = at("wdb", [128, HT, D], BF16, 0)
    xbuf = [at(f"xbuf{i}", [128, D], F32, 32 * KB + i * 4 * KB) for i in range(6)]
    xnbuf = [at(f"xnbuf{i}", [128, D], F32, 56 * KB + i * 4 * KB) for i in range(3)]
    sqjunk = at("sqjunk", [128, D], F32, 68 * KB)
    stage_hoh = [at(f"stg_hoh{i}", [128, KT * CH], BF16, 72 * KB + i * 2 * KB)
                 for i in range(2)]
    stage_hol = [at(f"stg_hol{i}", [128, KT * CH], BF16, 76 * KB + i * 2 * KB)
                 for i in range(2)]
    ho2h = [at(f"ho2h{i}", [128, KT, TGS], BF16, z1sz + 48 * KB + i * 8 * KB)
            for i in range(2)]
    ho2l = [at(f"ho2l{i}", [128, KT, TGS], BF16, z1sz + 64 * KB + i * 8 * KB)
            for i in range(2)]
    xn2h = [at(f"xn2h{i}", [128, 512], BF16, z1sz + 80 * KB + i * KB)
            for i in range(2)]
    xn2l = [at(f"xn2l{i}", [128, 512], BF16, z1sz + 82 * KB + i * KB)
            for i in range(2)]
    xn3h = [at(f"xn3h{i}", [128, KT, TGS], BF16, 48 * KB + i * KT * TGS * 2)
            for i in range(2)]
    xn3l = [at(f"xn3l{i}", [128, KT, TGS], BF16, 64 * KB + i * KT * TGS * 2)
            for i in range(2)]
    sg_st = [at(f"sg_st{i}", [128, 512], F32, 80 * KB + i * 2 * KB) for i in range(2)]
    h32_st = [at(f"h32_st{i}", [128, 512], F32, 84 * KB + i * 2 * KB) for i in range(2)]
    hh_st = [at(f"hh_st{i}", [128, 512], BF16, 88 * KB + i * KB) for i in range(2)]
    hl_st = [at(f"hl_st{i}", [128, 512], BF16, 90 * KB + i * KB) for i in range(2)]
    g_st = [at(f"g_st{i}", [128, 512], F32, 48 * KB + i * 2 * KB) for i in range(2)]
    out_st = [at(f"out_st{i}", [128, 512], F32, 52 * KB + i * 2 * KB) for i in range(2)]
    s_in = [at(f"s_in{i}", [128, 512], F32, 56 * KB + i * 2 * KB) for i in range(2)]
    x_in = [at(f"x_in{i}", [128, 512], F32, 60 * KB + i * 2 * KB) for i in range(2)]

    convb = at("convb", [128, KT, D], F32, z2)
    corrtb = at("corrtb", [128, KT, D], F32, z2 + 32 * KB)
    holo32 = [at(f"holo32_{i}", [128, KT * CH], F32, 80 * KB + i * 4 * KB)
              for i in range(2)]
    wgb = at("wgb", [128, KT, H], BF16, z2)
    wub = at("wub", [128, KT, H], BF16, z2 + 48 * KB)
    hh3 = [at(f"hh3_{i}", [128, HT, TGS], BF16, z2 + i * 24 * KB) for i in range(2)]
    hl3 = [at(f"hl3_{i}", [128, HT, TGS], BF16, z2 + 48 * KB + i * 24 * KB)
           for i in range(2)]

    identf = at("identf", [128, 128], F32, z0)
    o = z0 + 512
    cols = {}
    _ring_cols = [p + s for p in ["s1", "s2", "m", "ve", "sq", "rs", "nm"]
                  for s in "abcd"]
    for cn in (["gcol", "dcolin", "dcol", "ecol", "m2", "ta", "t1", "t2",
                "r0", "gap1", "gap2"] + _ring_cols):
        cols[cn] = at(cn, [128, 1], F32, o)
        o += 32
    o = (o + 31) // 32 * 32
    stage_xnh = [at(f"stg_xnh{i}", [128, KT * CH], BF16, o + i * 2 * KB)
                 for i in range(2)]
    o += 4 * KB
    stage_xnl = [at(f"stg_xnl{i}", [128, KT * CH], BF16, o + i * 2 * KB)
                 for i in range(2)]
    o += 4 * KB
    s_stage = [at(f"s_stage{i}", [128, 512], F32, o + i * 2 * KB) for i in range(2)]
    o += 4 * KB
    assert SBASE + o <= 229376, f"SBUF overflow: {SBASE + o}"

    pp = [nc.alloc_psum_tensor(f"pp{i}", [128, 1024], F32).ap() for i in range(3)]
    pb = nc.alloc_psum_tensor("pb", [128, 1024], F32).ap()
    pt = ([pp[i // 2][:, (i % 2) * 512:(i % 2) * 512 + 512] for i in range(6)]
          + [pb[:, 0:512], pb[:, 512:1024]])

    import contextlib
    es = contextlib.ExitStack()
    sems = {}
    sem_names = (["q_c", "q_bld", "q_wgu", "q_wd", "q_dbg",
                  "e_v", "e_s", "e_t"]
                 + [f"q_x{i}" for i in range(6)]
                 + [f"q_in2_{i}" for i in range(2)]
                 + [f"q_ho2_{i}" for i in range(2)]
                 + [f"q_spx{i}" for i in range(2)]
                 + [f"q_sps{i}" for i in range(2)]
                 + [f"q_sph{i}" for i in range(2)]
                 + [f"q_in3a{i}" for i in range(2)]
                 + [f"q_h3{i}" for i in range(2)]
                 + [f"q_sx{i}" for i in range(2)]
                 + [f"q_out{i}" for i in range(2)])
    for sn in sem_names:
        sems[sn] = es.enter_context(nc.semaphore(sn))
    C = {sn: 0 for sn in sems}

    OPS = {"sync": [], "vector": [], "scalar": [], "tensor": [], "gpsimd": []}

    def emit(engn, fn):
        OPS[engn].append(fn)

    def wait(engn, sn, val):
        if val <= 0:
            return
        emit(engn, lambda e, s=sems[sn], v=val: e.wait_ge(s, v))

    def bump(sn, by=1):
        C[sn] += by
        return C[sn]

    def op(engn, sn, by, fn):
        v = bump(sn, by)
        emit(engn, lambda e, f=fn, s=sems[sn], b=by: f(e).then_inc(s, b))
        return v

    def selfsync(engn, sn):
        wait(engn, sn, C[sn])

    op("sync", "q_c", 16, lambda e: e.dma_start(out=identf, in_=identf_d))
    op("sync", "q_c", 16, lambda e: e.dma_start(out=cols["gcol"], in_=gcol_d))
    op("sync", "q_c", 16, lambda e: e.dma_start(out=cols["dcolin"], in_=dcol_d))
    consts_q = C["q_c"]

    wait("scalar", "q_c", consts_q)
    e_exp = op("scalar", "e_s", 1, lambda e: e.activation(
        out=cols["ecol"], in_=cols["dcolin"], func=AF.Exp, scale=-1.0))
    wait("vector", "e_s", e_exp)
    op("vector", "e_v", 1, lambda e: e.tensor_scalar_add(
        out=cols["ta"], in0=cols["ecol"], scalar1=1.0))
    selfsync("vector", "e_v")
    e_dcol = op("vector", "e_v", 1, lambda e: e.reciprocal(out=cols["dcol"],
                                                           in_=cols["ta"]))
    dcol_bc = cols["dcol"][:, :1].broadcast_to((128, CH))

    def load_x(c):
        return op("sync", f"q_x{c % 6}", 16, lambda e, c=c: e.dma_start(
            out=xbuf[c % 6][:, :], in_=xw[c * CH:(c + 1) * CH, :]))

    x_q = {}
    conv_v = conv.rearrange("(k p) i -> p k i", p=128)
    corrt_v = corrt.rearrange("(k p) j -> p k j", p=128)
    op("sync", "q_bld", 16, lambda e: e.dma_start(out=convb[:, :, :], in_=conv_v))
    op("sync", "q_bld", 16, lambda e: e.dma_start(out=corrtb[:, :, :], in_=corrt_v))
    build_in_q = C["q_bld"]

    norm_done = {}
    sq_done = {}
    vareps_done = {}
    sqrt_done = {}
    stats_done = {}
    tp_done = {}
    scan_done = {}
    xnh_done = {}
    hoh_done = {}
    rel_done = {}
    chunkv_done = {}
    spx_cnt = {}
    bu_mm_done = {}
    bu_mth_done = {}
    bu_split_done = {}

    n_build = 16
    bu_list = [(mi, nj) for nj in range(2) for mi in range(NT)]
    next_bu = 0
    pending_splits = []

    def emit_build_splits(upto=10**9):
        while pending_splits and pending_splits[0][0] <= upto:
            u, mi, nj, bank = pending_splits.pop(0)
            wait("scalar", "e_t", bu_mm_done[u])
            bu_mth_done[u] = op("scalar", "e_s", 1,
                                lambda e, mi=mi, nj=nj, bank=bank: e.activation(
                out=mth[:, mi, nj * 512:(nj + 1) * 512],
                in_=pt[bank][:, 0:512],
                func=AF.Identity, scale=cols["gcol"][:, :1]))
            wait("vector", "e_s", bu_mth_done[u])
            selfsync("vector", "e_v")
            op("vector", "e_v", 1, lambda e, bank=bank: e.tensor_scalar_mul(
                out=s_stage[0][:, 0:512], in0=pt[bank][:, 0:512],
                scalar1=cols["gcol"][:, :1]))
            selfsync("vector", "e_v")
            bu_split_done[u] = op("vector", "e_v", 1,
                                  lambda e, mi=mi, nj=nj: e.tensor_tensor(
                out=mtl[:, mi, nj * 512:(nj + 1) * 512],
                in0=s_stage[0][:, 0:512],
                in1=mth[:, mi, nj * 512:(nj + 1) * 512], op=ALU.subtract))

    for c in range(min(6, NCH)):
        x_q[c] = load_x(c)

    def sfx_of(c):
        return "abcd"[c % 4]

    ho2_q = {}
    p2_slab_next = [0]

    def emit_p2_slab(tg):
        wait("sync", "e_t", bu_mm_done[n_build - 1])
        wait("sync", "q_spx0", spx_cnt[4 * tg + 2])
        wait("sync", "q_spx1", spx_cnt[4 * tg + 3])
        if tg >= 2:
            wait("sync", "e_t", ap_done[(tg - 1) * NT - 1])
        op("sync", f"q_ho2_{tg % 2}", 16, lambda e, tg=tg: e.dma_start(
            out=ho2h[tg % 2][:, :, :], in_=hoh_v[:, :, tg * TGS:(tg + 1) * TGS]))
        ho2_q[tg] = op("sync", f"q_ho2_{tg % 2}", 16, lambda e, tg=tg: e.dma_start(
            out=ho2l[tg % 2][:, :, :], in_=hol_v[:, :, tg * TGS:(tg + 1) * TGS]))

    ap_done = {}

    NSTAGE = 10
    for t in range(NCH + NSTAGE):
        emit_build_splits()
        tgn = p2_slab_next[0]
        if (tgn < min(2, NTG) and next_bu == n_build
                and (4 * tgn + 3 - WCH + WCH) in ()
                or False):
            pass
        if (tgn < min(2, NTG) and next_bu == n_build
                and (4 * tgn + 2) in spx_cnt and (4 * tgn + 3) in spx_cnt):
            emit_p2_slab(tgn)
            p2_slab_next[0] = tgn + 1
        c = t
        if 0 <= c < NCH:
            sfx = sfx_of(c)
            wait("scalar", f"q_x{c % 6}", x_q[c])
            selfsync("scalar", "e_s")
            sq_done[c] = op("scalar", "e_s", 1,
                            lambda e, c=c, sfx=sfx: e.activation(
                out=sqjunk[:, :], in_=xbuf[c % 6][:, :], func=AF.Square,
                accum_out=cols["s2" + sfx][:, :1]))

        c = t - 1
        if 0 <= c < NCH:
            sfx = sfx_of(c)
            wait("vector", f"q_x{c % 6}", x_q[c])
            op("vector", "e_v", 1, lambda e, c=c, sfx=sfx: e.reduce_sum(
                out=cols["s1" + sfx][:, :1], in_=xbuf[c % 6][:, :], axis=AX.X))
            wait("vector", "e_s", sq_done[c])
            selfsync("vector", "e_v")
            op("vector", "e_v", 1, lambda e, sfx=sfx: e.tensor_scalar_mul(
                out=cols["m" + sfx], in0=cols["s1" + sfx], scalar1=1.0 / D))
            emit("vector", lambda e, sfx=sfx: e.tensor_scalar(
                out=cols["ta"], in0=cols["s2" + sfx], scalar1=1.0 / D,
                scalar2=1e-5, op0=ALU.mult, op1=ALU.add))
            selfsync("vector", "e_v")
            op("vector", "e_v", 1, lambda e, sfx=sfx: e.tensor_scalar(
                out=cols["m2"], in0=cols["m" + sfx],
                scalar1=cols["m" + sfx][:, :1], scalar2=None, op0=ALU.mult))
            selfsync("vector", "e_v")
            vareps_done[c] = op("vector", "e_v", 1,
                                lambda e, sfx=sfx: e.scalar_tensor_tensor(
                out=cols["ve" + sfx], in0=cols["m2"], scalar=-1.0,
                in1=cols["ta"], op0=ALU.mult, op1=ALU.add))

        c = t - 2
        if 0 <= c < NCH:
            sfx = sfx_of(c)
            wait("scalar", "e_v", vareps_done[c])
            sqrt_done[c] = op("scalar", "e_s", 1,
                              lambda e, sfx=sfx: e.activation(
                out=cols["sq" + sfx], in_=cols["ve" + sfx], func=AF.Sqrt))

        c = t - 3
        if 0 <= c < NCH:
            sfx = sfx_of(c)
            wait("vector", "e_s", sqrt_done[c])
            op("vector", "e_v", 1, lambda e, sfx=sfx: e.reciprocal(
                out=cols["r0"], in_=cols["sq" + sfx]))
            selfsync("vector", "e_v")
            op("vector", "e_v", 1, lambda e: e.tensor_scalar(
                out=cols["t1"], in0=cols["r0"], scalar1=cols["r0"][:, :1],
                scalar2=None, op0=ALU.mult))
            selfsync("vector", "e_v")
            op("vector", "e_v", 1, lambda e, sfx=sfx: e.tensor_scalar(
                out=cols["t2"], in0=cols["t1"], scalar1=cols["ve" + sfx][:, :1],
                scalar2=-0.5, op0=ALU.mult, op1=ALU.mult))
            selfsync("vector", "e_v")
            op("vector", "e_v", 1, lambda e, sfx=sfx: e.scalar_tensor_tensor(
                out=cols["rs" + sfx], in0=cols["t2"], scalar=1.5,
                in1=cols["r0"], op0=ALU.add, op1=ALU.mult))
            selfsync("vector", "e_v")
            stats_done[c] = op("vector", "e_v", 1,
                               lambda e, sfx=sfx: e.scalar_tensor_tensor(
                out=cols["nm" + sfx], in0=cols["m" + sfx], scalar=-1.0,
                in1=cols["rs" + sfx], op0=ALU.mult, op1=ALU.mult))

        c = t - 4
        if 0 <= c < NCH:
            sfx = sfx_of(c)
            wait("scalar", "e_v", stats_done[c])
            wait("scalar", "e_t", tp_done.get(c - 3, 0))
            norm_done[c] = op("scalar", "e_s", 1,
                              lambda e, c=c, sfx=sfx: e.activation(
                out=xnbuf[c % 3][:, :], in_=xbuf[c % 6][:, :], func=AF.Identity,
                scale=cols["rs" + sfx][:, :1], bias=cols["nm" + sfx][:, :1]))
            if debug and c == WCH:
                op("scalar", "q_dbg", 16, lambda e, c=c: e.dma_start(
                    out=dbg_x, in_=xbuf[c % 6][:, :]))
                op("scalar", "q_dbg", 16, lambda e, c=c: e.dma_start(
                    out=dbg_xn, in_=xnbuf[c % 3][:, :]))

        c = t + 1
        if 6 <= c < NCH:
            wait("sync", "e_s", norm_done[c - 6])
            x_q[c] = load_x(c)

        c = t - 8
        if 0 <= c < NCH:
            oc = c - WCH
            if oc >= 0:
                wait("vector", "e_s", hoh_done[c])
                wait("vector", "e_v", chunkv_done.get(c - 2, 0))
                emit("vector", lambda e, c=c: e.tensor_tensor(
                    out=stage_xnl[(c - WCH) % 2][:, :], in0=pp[c % 3][:, :],
                    in1=stage_xnh[(c - WCH) % 2][:, :], op=ALU.subtract))
                selfsync("vector", "e_v")
                chunkv_done[c] = op("vector", "e_v", 1,
                                    lambda e, c=c, oc=oc: e.tensor_tensor(
                    out=stage_hol[oc % 2][:, :], in0=holo32[c % 2][:, :],
                    in1=stage_hoh[oc % 2][:, :], op=ALU.subtract))
                rel_done[c] = chunkv_done[c]

        if 0 <= c < NCH:
            oc = c - WCH
            if oc >= 0:
                wait("scalar", "e_v", chunkv_done[c])
                for scr_v, stg in ((xnh_v, stage_xnh), (xnl_v, stage_xnl),
                                   (hoh_v, stage_hoh), (hol_v, stage_hol)):
                    spx_cnt[oc] = op("scalar", f"q_spx{oc % 2}", 16,
                                     lambda e, oc=oc, scr_v=scr_v, stg=stg: e.dma_start(
                        out=scr_v[:, :, oc * CH:(oc + 1) * CH],
                        in_=stg[oc % 2].rearrange("p (k t) -> p k t", k=KT)))

        c = t - 5
        if 0 <= c < NCH:
            wait("tensor", "e_s", norm_done[c])
            wait("tensor", "q_c", consts_q)
            if c - 3 >= 0:
                assert (c - 3) in rel_done, f"rel_done missing for {c - 3}"
                if c - 3 >= WCH:
                    assert (c - 3) in xnh_done
            wait("tensor", "e_s", xnh_done.get(c - 3, 0))
            wait("tensor", "e_v", rel_done.get(c - 3, 0))
            p0, p1 = 2 * (c % 3), 2 * (c % 3) + 1
            for k in range(KT):
                bank, sl = (p0, k) if k < 4 else (p1, k - 4)
                fn = lambda e, c=c, k=k, bank=bank, sl=sl: e.transpose(
                    out=pt[bank][:, sl * 128:(sl + 1) * 128],
                    in_=xnbuf[c % 3][:, k * 128:(k + 1) * 128], identity=identf)
                if k == KT - 1:
                    tp_done[c] = op("tensor", "e_t", 1, fn)
                else:
                    emit("tensor", fn)

            if c >= BUILD_START and next_bu < n_build:
                n_units = (n_build - next_bu) if c == NCH - 1 else 2
                for _ in range(n_units):
                    if next_bu >= n_build:
                        break
                    u = next_bu
                    mi, nj = bu_list[u]
                    emit_build_splits(u - 2)
                    wait("tensor", "q_bld", build_in_q)
                    wait("tensor", "e_v", bu_split_done.get(u - 2, 0))
                    bank = 6 + u % 2
                    for k in range(KT):
                        fn = lambda e, k=k, mi=mi, nj=nj, bank=bank: e.matmul(
                            out=pt[bank][:, 0:512],
                            lhsT=convb[:, k, mi * 128:(mi + 1) * 128],
                            rhs=corrtb[:, k, nj * 512:(nj + 1) * 512],
                            start=(k == 0), stop=(k == KT - 1))
                        if k == KT - 1:
                            bu_mm_done[u] = op("tensor", "e_t", 1, fn)
                        else:
                            emit("tensor", fn)
                    pending_splits.append((u, mi, nj, bank))
                    next_bu += 1

        c = t - 6
        if 0 <= c < NCH:
            wait("vector", "e_t", tp_done[c])
            wait("vector", "e_s", hoh_done.get(c - 2, 0))
            wait("vector", "e_v", scan_done.get(c - 1, 0))
            wait("vector", "e_v", chunkv_done.get(c - 2, 0))
            if c == 0:
                wait("vector", "e_v", e_dcol)
            p0, p1 = 2 * (c % 3), 2 * (c % 3) + 1
            for k in range(KT):
                bank, sl = (p0, k) if k < 4 else (p1, k - 4)
                init = 0.0 if c == 0 else holo32[(c - 1) % 2][:, k * CH + CH - 1:
                                                              k * CH + CH]
                fn = lambda e, c=c, k=k, bank=bank, sl=sl, init=init: (
                    e.tensor_tensor_scan(
                        out=holo32[c % 2][:, k * CH:(k + 1) * CH], data0=dcol_bc,
                        data1=pt[bank][:, sl * 128:(sl + 1) * 128], initial=init,
                        op0=ALU.mult, op1=ALU.add))
                if k == KT - 1:
                    scan_done[c] = op("vector", "e_v", 1, fn)
                else:
                    emit("vector", fn)
            if c < WCH:
                rel_done[c] = scan_done[c]

        c = t - 7
        if 0 <= c < NCH:
            oc = c - WCH
            if oc >= 0:
                wait("scalar", "e_t", tp_done[c])
                wait("scalar", "e_v", scan_done[c])
                wait("scalar", f"q_spx{oc % 2}", spx_cnt.get(oc - 2, 0))
                selfsync("scalar", "e_s")
                xnh_done[c] = op("scalar", "e_s", 1,
                                 lambda e, c=c, oc=oc: e.copy(
                    out=stage_xnh[oc % 2][:, :], in_=pp[c % 3][:, :]))
                hoh_done[c] = op("scalar", "e_s", 1,
                                 lambda e, c=c, oc=oc: e.copy(
                    out=stage_hoh[oc % 2][:, :], in_=holo32[c % 2][:, :]))
            else:
                hoh_done[c] = 0

    emit_build_splits()
    assert next_bu == n_build
    P1_ev, P1_es, P1_et = C["e_v"], C["e_s"], C["e_t"]
    P1_spx = (C["q_spx0"], C["q_spx1"])

    wait("gpsimd", "e_t", bu_mm_done[n_build - 1])
    for k in range(KT):
        op("gpsimd", "q_wgu", 16, lambda e, k=k: e.dma_start(
            out=wgb[:, k, :], in_=wgT[k * 128:(k + 1) * 128, :]))

    wait("tensor", "e_v", bu_split_done[n_build - 1])
    wait("vector", "e_t", P1_et)
    wait("scalar", "e_v", P1_ev)

    sev_done = {}
    in2_q = {}
    sps_cnt = {}
    u = 0
    for tg in range(NTG):
        if p2_slab_next[0] <= tg:
            emit_p2_slab(tg)
            p2_slab_next[0] = tg + 1
        for n in range(NT):
            wait("sync", "e_v", sev_done.get(u - 2, 0))
            op("sync", f"q_in2_{u % 2}", 16, lambda e, u=u, n=n, tg=tg: e.dma_start(
                out=xn2h[u % 2][:, 0:512], in_=xnh_v[:, n, tg * TGS:(tg + 1) * TGS]))
            in2_q[u] = op("sync", f"q_in2_{u % 2}", 16, lambda e, u=u, n=n, tg=tg: e.dma_start(
                out=xn2l[u % 2][:, 0:512], in_=xnl_v[:, n, tg * TGS:(tg + 1) * TGS]))

            bank = 6 + u % 2
            wait("tensor", "e_v", sev_done.get(u - 2, 0))
            wait("tensor", f"q_ho2_{tg % 2}", ho2_q[tg])
            nmm = 0
            for (lh, rh) in ((mth, ho2h), (mth, ho2l), (mtl, ho2h)):
                for k in range(KT):
                    nmm += 1
                    fn = lambda e, lh=lh, rh=rh, k=k, n=n, tg=tg, bank=bank, \
                        st=(nmm == 1), sp=(nmm == 24): e.matmul(
                        out=pt[bank][:, 0:512], lhsT=lh[:, k, n * 128:(n + 1) * 128],
                        rhs=rh[tg % 2][:, k, :], start=st, stop=sp)
                    if nmm == 24:
                        ap_done[u] = op("tensor", "e_t", 1, fn)
                    else:
                        emit("tensor", fn)

            wait("vector", "e_t", ap_done[u])
            wait("vector", f"q_in2_{u % 2}", in2_q[u])
            wait("vector", f"q_sps{u % 2}", sps_cnt.get(u - 2, 0))
            wait("vector", "e_v", sev_done.get(u - 2, 0))
            op("vector", "e_v", 1, lambda e, u=u, bank=bank: e.tensor_tensor(
                out=s_stage[u % 2][:, 0:512], in0=pt[bank][:, 0:512],
                in1=xn2h[u % 2][:, 0:512], op=ALU.add))
            selfsync("vector", "e_v")
            sev_done[u] = op("vector", "e_v", 1, lambda e, u=u: e.tensor_tensor(
                out=s_stage[u % 2][:, 0:512], in0=s_stage[u % 2][:, 0:512],
                in1=xn2l[u % 2][:, 0:512], op=ALU.add))

            wait("scalar", "e_v", sev_done[u])
            sps_cnt[u] = op("scalar", f"q_sps{u % 2}", 16,
                            lambda e, u=u, n=n, tg=tg: e.dma_start(
                out=s_scr[n * 128:(n + 1) * 128, tg * TGS:(tg + 1) * TGS],
                in_=s_stage[u % 2][:, 0:512]))
            u += 1

    P2_ev, P2_es, P2_et = C["e_v"], C["e_s"], C["e_t"]
    P2_sps = (C["q_sps0"], C["q_sps1"])

    wait("gpsimd", "e_v", P2_ev)
    wait("gpsimd", "e_t", P2_et)
    for k in range(KT):
        op("gpsimd", "q_wgu", 16, lambda e, k=k: e.dma_start(
            out=wub[:, k, :], in_=wuT[k * 128:(k + 1) * 128, :]))
    W_gu = C["q_wgu"]
    for k in range(HT):
        op("gpsimd", "q_wd", 16, lambda e, k=k: e.dma_start(
            out=wdb[:, k, :], in_=wdT[k * 128:(k + 1) * 128, :]))
    W_d = C["q_wd"]

    wait("tensor", "e_v", P2_ev)
    wait("tensor", "q_wgu", W_gu)
    wait("vector", "e_t", P2_et)
    wait("scalar", "e_v", P2_ev)
    wait("sync", "e_v", P2_ev)

    in3a_q = {}
    gate_done = {}
    up_done = {}
    sg_done = {}
    h32_done = {}
    hl_done = {}
    sph_cnt = {}
    u = 0
    for tg in range(NTG):
        if tg >= 2:
            wait("sync", "e_t", up_done.get((tg - 1) * HT - 1, 0))
        op("sync", f"q_in3a{tg % 2}", 16, lambda e, tg=tg: e.dma_start(
            out=xn3h[tg % 2][:, :, :], in_=xnh_v[:, :, tg * TGS:(tg + 1) * TGS]))
        in3a_q[tg] = op("sync", f"q_in3a{tg % 2}", 16, lambda e, tg=tg: e.dma_start(
            out=xn3l[tg % 2][:, :, :], in_=xnl_v[:, :, tg * TGS:(tg + 1) * TGS]))

        for m in range(HT):
            bA = u % 2
            bB = 2 + u % 2
            wait("tensor", f"q_in3a{tg % 2}", in3a_q[tg])
            wait("tensor", "e_s", sg_done.get(u - 2, 0))
            nmm = 0
            for src in (xn3h, xn3l):
                for k in range(KT):
                    nmm += 1
                    fn = lambda e, src=src, k=k, m=m, tg=tg, bA=bA, \
                        st=(nmm == 1), sp=(nmm == 16): e.matmul(
                        out=pt[bA][:, 0:512], lhsT=wgb[:, k, m * 128:(m + 1) * 128],
                        rhs=src[tg % 2][:, k, :], start=st, stop=sp)
                    if nmm == 16:
                        gate_done[u] = op("tensor", "e_t", 1, fn)
                    else:
                        emit("tensor", fn)
            wait("tensor", "e_v", h32_done.get(u - 2, 0))
            nmm = 0
            for src in (xn3h, xn3l):
                for k in range(KT):
                    nmm += 1
                    fn = lambda e, src=src, k=k, m=m, tg=tg, bB=bB, \
                        st=(nmm == 1), sp=(nmm == 16): e.matmul(
                        out=pt[bB][:, 0:512], lhsT=wub[:, k, m * 128:(m + 1) * 128],
                        rhs=src[tg % 2][:, k, :], start=st, stop=sp)
                    if nmm == 16:
                        up_done[u] = op("tensor", "e_t", 1, fn)
                    else:
                        emit("tensor", fn)

            wait("scalar", "e_t", gate_done[u])
            wait("scalar", "e_v", h32_done.get(u - 2, 0))
            sg_done[u] = op("scalar", "e_s", 1, lambda e, u=u, bA=bA: e.activation(
                out=sg_st[u % 2][:, 0:512], in_=pt[bA][:, 0:512], func=AF.Silu,
                scale=1.0 / 127.0))
            wait("vector", "e_t", up_done[u])
            wait("vector", "e_s", sg_done[u])
            h32_done[u] = op("vector", "e_v", 1, lambda e, u=u, bB=bB: e.scalar_tensor_tensor(
                out=h32_st[u % 2][:, 0:512], in0=pt[bB][:, 0:512], scalar=1.0 / 127.0,
                in1=sg_st[u % 2][:, 0:512], op0=ALU.mult, op1=ALU.mult))
            wait("scalar", "e_v", h32_done[u])
            wait("scalar", f"q_sph{u % 2}", sph_cnt.get(u - 2, 0))
            hh_done = op("scalar", "e_s", 1, lambda e, u=u: e.copy(
                out=hh_st[u % 2][:, 0:512], in_=h32_st[u % 2][:, 0:512]))
            wait("vector", "e_s", hh_done)
            hl_done[u] = op("vector", "e_v", 1, lambda e, u=u: e.tensor_tensor(
                out=hl_st[u % 2][:, 0:512], in0=h32_st[u % 2][:, 0:512],
                in1=hh_st[u % 2][:, 0:512], op=ALU.subtract))
            wait("scalar", "e_v", hl_done[u])
            op("scalar", f"q_sph{u % 2}", 16, lambda e, u=u, m=m, tg=tg: e.dma_start(
                out=hh_scr[m * 128:(m + 1) * 128, tg * TGS:(tg + 1) * TGS],
                in_=hh_st[u % 2][:, 0:512]))
            sph_cnt[u] = op("scalar", f"q_sph{u % 2}", 16,
                            lambda e, u=u, m=m, tg=tg: e.dma_start(
                out=hl_scr[m * 128:(m + 1) * 128, tg * TGS:(tg + 1) * TGS],
                in_=hl_st[u % 2][:, 0:512]))
            u += 1

    P3a_ev, P3a_es, P3a_et = C["e_v"], C["e_s"], C["e_t"]
    P3a_sph = (C["q_sph0"], C["q_sph1"])

    wait("tensor", "e_v", P3a_ev)
    wait("tensor", "q_wd", W_d)
    wait("vector", "e_t", P3a_et)
    wait("scalar", "e_v", P3a_ev)
    wait("sync", "e_t", P3a_et)
    wait("gpsimd", "e_t", P3a_et)
    wait("gpsimd", "e_v", P3a_ev)
    wait("sync", "q_sph0", P3a_sph[0])
    wait("sync", "q_sph1", P3a_sph[1])
    wait("sync", "q_sps0", P2_sps[0])
    wait("sync", "q_sps1", P2_sps[1])

    in3b_q = {}
    sx_q = {}
    dn_done = {}
    g_done = {}
    out_done = {}
    out_cnt = {}
    u = 0
    for tg in range(NTG):
        if tg >= 2:
            wait("sync", "e_t", dn_done.get((tg - 1) * NT - 1, 0))
        op("sync", f"q_h3{tg % 2}", 16, lambda e, tg=tg: e.dma_start(
            out=hh3[tg % 2][:, :, :], in_=hh_v[:, :, tg * TGS:(tg + 1) * TGS]))
        in3b_q[tg] = op("sync", f"q_h3{tg % 2}", 16, lambda e, tg=tg: e.dma_start(
            out=hl3[tg % 2][:, :, :], in_=hl_v[:, :, tg * TGS:(tg + 1) * TGS]))

        for n in range(NT):
            bank = 4 + u % 4
            wait("gpsimd", "e_v", out_done.get(u - 2, 0))
            op("gpsimd", f"q_sx{u % 2}", 16, lambda e, u=u, n=n, tg=tg: e.dma_start(
                out=s_in[u % 2][:, 0:512],
                in_=s_scr[n * 128:(n + 1) * 128, tg * TGS:(tg + 1) * TGS]))
            sx_q[u] = op("gpsimd", f"q_sx{u % 2}", 16, lambda e, u=u, n=n, tg=tg: e.dma_start(
                out=x_in[u % 2][:, 0:512],
                in_=xT[n * 128:(n + 1) * 128, tg * TGS:(tg + 1) * TGS]))

            wait("tensor", f"q_h3{tg % 2}", in3b_q[tg])
            wait("tensor", "e_s", g_done.get(u - 4, 0))
            nmm = 0
            for src in (hh3, hl3):
                for k in range(HT):
                    nmm += 1
                    fn = lambda e, src=src, k=k, n=n, tg=tg, bank=bank, \
                        st=(nmm == 1), sp=(nmm == 48): e.matmul(
                        out=pt[bank][:, 0:512], lhsT=wdb[:, k, n * 128:(n + 1) * 128],
                        rhs=src[tg % 2][:, k, :], start=st, stop=sp)
                    if nmm == 48:
                        dn_done[u] = op("tensor", "e_t", 1, fn)
                    else:
                        emit("tensor", fn)

            wait("scalar", "e_t", dn_done[u])
            wait("scalar", "e_v", out_done.get(u - 2, 0))
            g_done[u] = op("scalar", "e_s", 1, lambda e, u=u, bank=bank: e.activation(
                out=g_st[u % 2][:, 0:512], in_=pt[bank][:, 0:512], func=AF.Silu,
                scale=1.0 / 127.0))
            wait("vector", "e_s", g_done[u])
            wait("vector", f"q_sx{u % 2}", sx_q[u])
            wait("vector", f"q_out{u % 2}", out_cnt.get(u - 2, 0))
            wait("vector", "e_v", out_done.get(u - 2, 0))
            op("vector", "e_v", 1, lambda e, u=u: e.tensor_tensor(
                out=out_st[u % 2][:, 0:512], in0=g_st[u % 2][:, 0:512],
                in1=s_in[u % 2][:, 0:512], op=ALU.mult))
            selfsync("vector", "e_v")
            out_done[u] = op("vector", "e_v", 1, lambda e, u=u: e.tensor_tensor(
                out=out_st[u % 2][:, 0:512], in0=out_st[u % 2][:, 0:512],
                in1=x_in[u % 2][:, 0:512], op=ALU.add))
            wait("scalar", "e_v", out_done[u])
            out_cnt[u] = op("scalar", f"q_out{u % 2}", 16,
                            lambda e, u=u, n=n, tg=tg: e.dma_start(
                out=outT[n * 128:(n + 1) * 128, tg * TGS:(tg + 1) * TGS],
                in_=out_st[u % 2][:, 0:512]))
            u += 1

    wait("scalar", "q_out0", C["q_out0"])
    wait("scalar", "q_out1", C["q_out1"])

    with nc.Block() as block:
        @block.sync
        def _(eng):
            for fn in OPS["sync"]:
                fn(eng)

        @block.vector
        def _(eng):
            for fn in OPS["vector"]:
                fn(eng)

        @block.scalar
        def _(eng):
            for fn in OPS["scalar"]:
                fn(eng)

        @block.tensor
        def _(eng):
            for fn in OPS["tensor"]:
                fn(eng)

        @block.gpsimd
        def _(eng):
            for fn in OPS["gpsimd"]:
                fn(eng)

    es.close()
    return nc


def kernel(L0_attn_x, L0_attn_bind_key, L0_attn_unbind_key, L0_attn_gate,
           L0_attn_decay, L0_ffn_w_gate, L0_ffn_w_up, L0_ffn_w_down):
    OWN, W_ = 2048, 512
    x = np.ascontiguousarray(np.asarray(L0_attn_x, dtype=np.float32))
    kb = np.asarray(L0_attn_bind_key, dtype=np.float32)
    ku = np.asarray(L0_attn_unbind_key, dtype=np.float32)
    gate = np.asarray(L0_attn_gate, dtype=np.float32).reshape(-1)[0]
    decay = np.asarray(L0_attn_decay, dtype=np.float32).reshape(())

    k_ = np.arange(D)[:, None]
    i_ = np.arange(D)[None, :]
    conv = np.ascontiguousarray(kb[(k_ - i_) % D])
    corrt = np.ascontiguousarray(ku[(k_ - i_) % D])
    wgT = np.ascontiguousarray(np.asarray(L0_ffn_w_gate, dtype=np.int32).T)
    wuT = np.ascontiguousarray(np.asarray(L0_ffn_w_up, dtype=np.int32).T)
    wdT = np.ascontiguousarray(np.asarray(L0_ffn_w_down, dtype=np.int32).T)
    gcol = np.full((128, 1), gate, np.float32)
    dcol = np.full((128, 1), decay, np.float32)
    identf = np.eye(128, dtype=np.float32)

    nc = build_nc(OWN, W_)
    in_maps = []
    for c in range(8):
        b, q = c // 4, c % 4
        t0 = q * OWN
        xw = np.zeros((W_ + OWN, D), np.float32)
        if q > 0:
            xw[:W_] = x[b, t0 - W_:t0]
        xw[W_:] = x[b, t0:t0 + OWN]
        in_maps.append(dict(
            xw=xw, xT=np.ascontiguousarray(x[b, t0:t0 + OWN].T),
            conv=conv, corrt=corrt, wgT=wgT, wuT=wuT, wdT=wdT,
            gcol=gcol, dcol=dcol, identf=identf))
    res = run_bass_kernel_spmd(nc, in_maps, core_ids=list(range(8)))
    out = np.empty_like(x)
    for c in range(8):
        b, q = c // 4, c % 4
        out[b, q * OWN:(q + 1) * OWN] = res.results[c]["outT"].T
    return out


# version 09
# speedup vs baseline: 1.0325x; 1.0048x over previous
import numpy as np
import concourse.bass as bass
import concourse.mybir as mybir
from concourse.bass_utils import run_bass_kernel_spmd

F32 = mybir.dt.float32
BF16 = mybir.dt.bfloat16
I32 = mybir.dt.int32
AF = mybir.ActivationFunctionType
ALU = mybir.AluOpType
AX = mybir.AxisListType

D = 1024
KT = 8
H = 3072
HT = 24
NT = 8
TGS = 512
CH = 128

KB = 1024


def build_nc(OWN=2048, W=512, debug=False):
    NCH = (W + OWN) // CH
    WCH = W // CH
    NTG = OWN // TGS
    BUILD_START = 1

    nc = bass.Bass("TRN2", target_bir_lowering=False)

    xw = nc.dram_tensor("xw", [W + OWN, D], F32, kind="ExternalInput").ap()
    xT = nc.dram_tensor("xT", [D, OWN], F32, kind="ExternalInput").ap()
    conv = nc.dram_tensor("conv", [D, D], F32, kind="ExternalInput").ap()
    corrt = nc.dram_tensor("corrt", [D, D], F32, kind="ExternalInput").ap()
    wgT = nc.dram_tensor("wgT", [D, H], I32, kind="ExternalInput").ap()
    wuT = nc.dram_tensor("wuT", [D, H], I32, kind="ExternalInput").ap()
    wdT = nc.dram_tensor("wdT", [H, D], I32, kind="ExternalInput").ap()
    gcol_d = nc.dram_tensor("gcol", [128, 1], F32, kind="ExternalInput").ap()
    dcol_d = nc.dram_tensor("dcol", [128, 1], F32, kind="ExternalInput").ap()
    identf_d = nc.dram_tensor("identf", [128, 128], F32, kind="ExternalInput").ap()

    outT = nc.dram_tensor("outT", [D, OWN], F32, kind="ExternalOutput").ap()
    if debug:
        dbg_x = nc.dram_tensor("dbg_x", [128, D], F32, kind="ExternalOutput").ap()
        dbg_xn = nc.dram_tensor("dbg_xn", [128, D], F32, kind="ExternalOutput").ap()

    _scrk = "ExternalOutput" if debug else "Internal"
    xnh_scr = nc.dram_tensor("xnh_scr", [D, OWN], BF16, kind=_scrk).ap()
    xnl_scr = nc.dram_tensor("xnl_scr", [D, OWN], BF16, kind=_scrk).ap()
    hoh_scr = nc.dram_tensor("hoh_scr", [D, OWN], BF16, kind="Internal").ap()
    hol_scr = nc.dram_tensor("hol_scr", [D, OWN], BF16, kind="Internal").ap()
    s_scr = nc.dram_tensor("s_scr", [D, OWN], F32, kind=_scrk).ap()
    hh_scr = nc.dram_tensor("hh_scr", [H, OWN], BF16, kind=_scrk).ap()
    hl_scr = nc.dram_tensor("hl_scr", [H, OWN], BF16, kind=_scrk).ap()

    hoh_v = hoh_scr.rearrange("(k p) t -> p k t", p=128)
    hol_v = hol_scr.rearrange("(k p) t -> p k t", p=128)
    xnh_v = xnh_scr.rearrange("(k p) t -> p k t", p=128)
    xnl_v = xnl_scr.rearrange("(k p) t -> p k t", p=128)
    hh_v = hh_scr.rearrange("(k p) t -> p k t", p=128)
    hl_v = hl_scr.rearrange("(k p) t -> p k t", p=128)

    SBASE = 16512
    z1sz = max(32 * KB + 2 * (KT * OWN * 2), 92 * KB)
    z2 = z1sz
    z0 = z1sz + 96 * KB

    def at(name, shape, dtype, off):
        return nc.alloc_sbuf_tensor_at(name, shape, dtype, offset=SBASE + off).ap()

    mth = at("mth", [128, KT, D], BF16, 0)
    mtl = at("mtl", [128, KT, D], BF16, 16 * KB)
    wdb_a = at("wdb_a", [128, 16, D], BF16, 0)
    wdb_b = at("wdb_b", [128, HT - 16, D], BF16, 64 * KB)
    xbuf = [at(f"xbuf{i}", [128, D], F32, 32 * KB + i * 4 * KB) for i in range(6)]
    xnbuf = [at(f"xnbuf{i}", [128, D], F32, 56 * KB + i * 4 * KB) for i in range(3)]
    sqjunk = at("sqjunk", [128, D], F32, 68 * KB)
    stage_hoh = [at(f"stg_hoh{i}", [128, KT * CH], BF16, 72 * KB + i * 2 * KB)
                 for i in range(2)]
    stage_hol = [at(f"stg_hol{i}", [128, KT * CH], BF16, 76 * KB + i * 2 * KB)
                 for i in range(2)]
    ho2h = [at(f"ho2h{i}", [128, KT, TGS], BF16, z1sz + 48 * KB + i * 8 * KB)
            for i in range(2)]
    ho2l = [at(f"ho2l{i}", [128, KT, TGS], BF16, z1sz + 64 * KB + i * 8 * KB)
            for i in range(2)]
    xn2h = [at(f"xn2h{i}", [128, 512], BF16, z1sz + 80 * KB + i * KB)
            for i in range(2)]
    xn2l = [at(f"xn2l{i}", [128, 512], BF16, z1sz + 82 * KB + i * KB)
            for i in range(2)]
    xn3h = [at(f"xn3h{i}", [128, KT, TGS], BF16, z1sz + 48 * KB + i * KT * TGS * 2)
            for i in range(2)]
    xn3l = [at(f"xn3l{i}", [128, KT, TGS], BF16, z1sz + 64 * KB + i * KT * TGS * 2)
            for i in range(2)]
    sg_st = [at(f"sg_st{i}", [128, 512], F32, 80 * KB + i * 2 * KB) for i in range(2)]
    h32_st = [at(f"h32_st{i}", [128, 512], F32, 84 * KB + i * 2 * KB) for i in range(2)]
    hh_st = [at(f"hh_st{i}", [128, 512], BF16, 88 * KB + i * KB) for i in range(2)]
    hl_st = [at(f"hl_st{i}", [128, 512], BF16, 90 * KB + i * KB) for i in range(2)]
    g_st = [at(f"g_st{i}", [128, 512], F32, 48 * KB + i * 2 * KB) for i in range(2)]
    out_st = [at(f"out_st{i}", [128, 512], F32, 52 * KB + i * 2 * KB) for i in range(2)]
    s_in = [at(f"s_in{i}", [128, 512], F32, 56 * KB + i * 2 * KB) for i in range(2)]
    x_in = [at(f"x_in{i}", [128, 512], F32, 60 * KB + i * 2 * KB) for i in range(2)]

    convb = at("convb", [128, KT, D], F32, z2)
    corrtb = at("corrtb", [128, KT, D], F32, z2 + 32 * KB)
    holo32 = [at(f"holo32_{i}", [128, KT * CH], F32, 80 * KB + i * 4 * KB)
              for i in range(2)]
    wgb = at("wgb", [128, KT, H], BF16, z2)
    wub = at("wub", [128, KT, H], BF16, 32 * KB)
    hh3 = [at(f"hh3_{i}", [128, HT, TGS], BF16, z2 + i * 24 * KB) for i in range(2)]
    hl3 = [at(f"hl3_{i}", [128, HT, TGS], BF16, z2 + 48 * KB + i * 24 * KB)
           for i in range(2)]

    identf = at("identf", [128, 128], F32, z0)
    o = z0 + 512
    cols = {}
    _ring_cols = [p + s for p in ["s1", "s2", "m", "ve", "sq", "rs", "nm"]
                  for s in "abcd"]
    for cn in (["gcol", "dcolin", "dcol", "ecol", "m2", "ta", "t1", "t2",
                "r0", "gap1", "gap2"] + _ring_cols):
        cols[cn] = at(cn, [128, 1], F32, o)
        o += 32
    o = (o + 31) // 32 * 32
    stage_xnh = [at(f"stg_xnh{i}", [128, KT * CH], BF16, o + i * 2 * KB)
                 for i in range(2)]
    o += 4 * KB
    stage_xnl = [at(f"stg_xnl{i}", [128, KT * CH], BF16, o + i * 2 * KB)
                 for i in range(2)]
    o += 4 * KB
    s_stage = [at(f"s_stage{i}", [128, 512], F32, o + i * 2 * KB) for i in range(2)]
    o += 4 * KB
    assert SBASE + o <= 229376, f"SBUF overflow: {SBASE + o}"

    pp = [nc.alloc_psum_tensor(f"pp{i}", [128, 1024], F32).ap() for i in range(3)]
    pb = nc.alloc_psum_tensor("pb", [128, 1024], F32).ap()
    pt = ([pp[i // 2][:, (i % 2) * 512:(i % 2) * 512 + 512] for i in range(6)]
          + [pb[:, 0:512], pb[:, 512:1024]])

    import contextlib
    es = contextlib.ExitStack()
    sems = {}
    sem_names = (["q_c", "q_bld", "q_wgu", "q_wd", "q_dbg",
                  "e_v", "e_s", "e_t"]
                 + [f"q_x{i}" for i in range(6)]
                 + [f"q_in2_{i}" for i in range(2)]
                 + [f"q_ho2_{i}" for i in range(2)]
                 + [f"q_spx{i}" for i in range(2)]
                 + [f"q_sps{i}" for i in range(2)]
                 + [f"q_sph{i}" for i in range(2)]
                 + [f"q_in3a{i}" for i in range(2)]
                 + [f"q_h3{i}" for i in range(2)]
                 + [f"q_sx{i}" for i in range(2)]
                 + [f"q_out{i}" for i in range(2)])
    for sn in sem_names:
        sems[sn] = es.enter_context(nc.semaphore(sn))
    C = {sn: 0 for sn in sems}

    OPS = {"sync": [], "vector": [], "scalar": [], "tensor": [], "gpsimd": []}

    def emit(engn, fn):
        OPS[engn].append(fn)

    def wait(engn, sn, val):
        if val <= 0:
            return
        emit(engn, lambda e, s=sems[sn], v=val: e.wait_ge(s, v))

    def bump(sn, by=1):
        C[sn] += by
        return C[sn]

    def op(engn, sn, by, fn):
        v = bump(sn, by)
        emit(engn, lambda e, f=fn, s=sems[sn], b=by: f(e).then_inc(s, b))
        return v

    def selfsync(engn, sn):
        wait(engn, sn, C[sn])

    op("sync", "q_c", 16, lambda e: e.dma_start(out=identf, in_=identf_d))
    op("sync", "q_c", 16, lambda e: e.dma_start(out=cols["gcol"], in_=gcol_d))
    op("sync", "q_c", 16, lambda e: e.dma_start(out=cols["dcolin"], in_=dcol_d))
    consts_q = C["q_c"]

    wait("scalar", "q_c", consts_q)
    e_exp = op("scalar", "e_s", 1, lambda e: e.activation(
        out=cols["ecol"], in_=cols["dcolin"], func=AF.Exp, scale=-1.0))
    wait("vector", "e_s", e_exp)
    op("vector", "e_v", 1, lambda e: e.tensor_scalar_add(
        out=cols["ta"], in0=cols["ecol"], scalar1=1.0))
    selfsync("vector", "e_v")
    e_dcol = op("vector", "e_v", 1, lambda e: e.reciprocal(out=cols["dcol"],
                                                           in_=cols["ta"]))
    dcol_bc = cols["dcol"][:, :1].broadcast_to((128, CH))

    def load_x(c):
        return op("sync", f"q_x{c % 6}", 16, lambda e, c=c: e.dma_start(
            out=xbuf[c % 6][:, :], in_=xw[c * CH:(c + 1) * CH, :]))

    x_q = {}
    conv_v = conv.rearrange("(k p) i -> p k i", p=128)
    corrt_v = corrt.rearrange("(k p) j -> p k j", p=128)
    op("sync", "q_bld", 16, lambda e: e.dma_start(out=convb[:, :, :], in_=conv_v))
    op("sync", "q_bld", 16, lambda e: e.dma_start(out=corrtb[:, :, :], in_=corrt_v))
    build_in_q = C["q_bld"]

    norm_done = {}
    sq_done = {}
    vareps_done = {}
    sqrt_done = {}
    stats_done = {}
    tp_done = {}
    scan_done = {}
    xnh_done = {}
    hoh_done = {}
    rel_done = {}
    chunkv_done = {}
    spx_cnt = {}
    bu_mm_done = {}
    bu_mth_done = {}
    bu_split_done = {}

    n_build = 16
    bu_list = [(mi, nj) for nj in range(2) for mi in range(NT)]
    next_bu = 0
    pending_splits = []

    def emit_build_splits(upto=10**9):
        while pending_splits and pending_splits[0][0] <= upto:
            u, mi, nj, bank = pending_splits.pop(0)
            wait("scalar", "e_t", bu_mm_done[u])
            bu_mth_done[u] = op("scalar", "e_s", 1,
                                lambda e, mi=mi, nj=nj, bank=bank: e.activation(
                out=mth[:, mi, nj * 512:(nj + 1) * 512],
                in_=pt[bank][:, 0:512],
                func=AF.Identity, scale=cols["gcol"][:, :1]))
            wait("vector", "e_s", bu_mth_done[u])
            selfsync("vector", "e_v")
            op("vector", "e_v", 1, lambda e, bank=bank: e.tensor_scalar_mul(
                out=s_stage[0][:, 0:512], in0=pt[bank][:, 0:512],
                scalar1=cols["gcol"][:, :1]))
            selfsync("vector", "e_v")
            bu_split_done[u] = op("vector", "e_v", 1,
                                  lambda e, mi=mi, nj=nj: e.tensor_tensor(
                out=mtl[:, mi, nj * 512:(nj + 1) * 512],
                in0=s_stage[0][:, 0:512],
                in1=mth[:, mi, nj * 512:(nj + 1) * 512], op=ALU.subtract))

    for c in range(min(6, NCH)):
        x_q[c] = load_x(c)

    def sfx_of(c):
        return "abcd"[c % 4]

    ho2_q = {}
    p2_slab_next = [0]

    def emit_p2_slab(tg):
        wait("sync", "e_t", bu_mm_done[n_build - 1])
        wait("sync", "q_spx0", spx_cnt[4 * tg + 2])
        wait("sync", "q_spx1", spx_cnt[4 * tg + 3])
        if tg >= 2:
            wait("sync", "e_t", ap_done[(tg - 1) * NT - 1])
        op("sync", f"q_ho2_{tg % 2}", 16, lambda e, tg=tg: e.dma_start(
            out=ho2h[tg % 2][:, :, :], in_=hoh_v[:, :, tg * TGS:(tg + 1) * TGS]))
        ho2_q[tg] = op("sync", f"q_ho2_{tg % 2}", 16, lambda e, tg=tg: e.dma_start(
            out=ho2l[tg % 2][:, :, :], in_=hol_v[:, :, tg * TGS:(tg + 1) * TGS]))

    ap_done = {}

    NSTAGE = 10
    for t in range(NCH + NSTAGE):
        emit_build_splits()
        tgn = p2_slab_next[0]
        if (tgn < min(2, NTG) and next_bu == n_build
                and (4 * tgn + 3 - WCH + WCH) in ()
                or False):
            pass
        if (tgn < min(2, NTG) and next_bu == n_build
                and (4 * tgn + 2) in spx_cnt and (4 * tgn + 3) in spx_cnt):
            emit_p2_slab(tgn)
            p2_slab_next[0] = tgn + 1
        c = t
        if 0 <= c < NCH:
            sfx = sfx_of(c)
            wait("scalar", f"q_x{c % 6}", x_q[c])
            selfsync("scalar", "e_s")
            sq_done[c] = op("scalar", "e_s", 1,
                            lambda e, c=c, sfx=sfx: e.activation(
                out=sqjunk[:, :], in_=xbuf[c % 6][:, :], func=AF.Square,
                accum_out=cols["s2" + sfx][:, :1]))

        c = t - 1
        if 0 <= c < NCH:
            sfx = sfx_of(c)
            wait("vector", f"q_x{c % 6}", x_q[c])
            op("vector", "e_v", 1, lambda e, c=c, sfx=sfx: e.reduce_sum(
                out=cols["s1" + sfx][:, :1], in_=xbuf[c % 6][:, :], axis=AX.X))
            wait("vector", "e_s", sq_done[c])
            selfsync("vector", "e_v")
            op("vector", "e_v", 1, lambda e, sfx=sfx: e.tensor_scalar_mul(
                out=cols["m" + sfx], in0=cols["s1" + sfx], scalar1=1.0 / D))
            emit("vector", lambda e, sfx=sfx: e.tensor_scalar(
                out=cols["ta"], in0=cols["s2" + sfx], scalar1=1.0 / D,
                scalar2=1e-5, op0=ALU.mult, op1=ALU.add))
            selfsync("vector", "e_v")
            op("vector", "e_v", 1, lambda e, sfx=sfx: e.tensor_scalar(
                out=cols["m2"], in0=cols["m" + sfx],
                scalar1=cols["m" + sfx][:, :1], scalar2=None, op0=ALU.mult))
            selfsync("vector", "e_v")
            vareps_done[c] = op("vector", "e_v", 1,
                                lambda e, sfx=sfx: e.scalar_tensor_tensor(
                out=cols["ve" + sfx], in0=cols["m2"], scalar=-1.0,
                in1=cols["ta"], op0=ALU.mult, op1=ALU.add))

        c = t - 2
        if 0 <= c < NCH:
            sfx = sfx_of(c)
            wait("scalar", "e_v", vareps_done[c])
            sqrt_done[c] = op("scalar", "e_s", 1,
                              lambda e, sfx=sfx: e.activation(
                out=cols["sq" + sfx], in_=cols["ve" + sfx], func=AF.Sqrt))

        c = t - 3
        if 0 <= c < NCH:
            sfx = sfx_of(c)
            wait("vector", "e_s", sqrt_done[c])
            op("vector", "e_v", 1, lambda e, sfx=sfx: e.reciprocal(
                out=cols["r0"], in_=cols["sq" + sfx]))
            selfsync("vector", "e_v")
            op("vector", "e_v", 1, lambda e: e.tensor_scalar(
                out=cols["t1"], in0=cols["r0"], scalar1=cols["r0"][:, :1],
                scalar2=None, op0=ALU.mult))
            selfsync("vector", "e_v")
            op("vector", "e_v", 1, lambda e, sfx=sfx: e.tensor_scalar(
                out=cols["t2"], in0=cols["t1"], scalar1=cols["ve" + sfx][:, :1],
                scalar2=-0.5, op0=ALU.mult, op1=ALU.mult))
            selfsync("vector", "e_v")
            op("vector", "e_v", 1, lambda e, sfx=sfx: e.scalar_tensor_tensor(
                out=cols["rs" + sfx], in0=cols["t2"], scalar=1.5,
                in1=cols["r0"], op0=ALU.add, op1=ALU.mult))
            selfsync("vector", "e_v")
            stats_done[c] = op("vector", "e_v", 1,
                               lambda e, sfx=sfx: e.scalar_tensor_tensor(
                out=cols["nm" + sfx], in0=cols["m" + sfx], scalar=-1.0,
                in1=cols["rs" + sfx], op0=ALU.mult, op1=ALU.mult))

        c = t - 4
        if 0 <= c < NCH:
            sfx = sfx_of(c)
            wait("scalar", "e_v", stats_done[c])
            wait("scalar", "e_t", tp_done.get(c - 3, 0))
            norm_done[c] = op("scalar", "e_s", 1,
                              lambda e, c=c, sfx=sfx: e.activation(
                out=xnbuf[c % 3][:, :], in_=xbuf[c % 6][:, :], func=AF.Identity,
                scale=cols["rs" + sfx][:, :1], bias=cols["nm" + sfx][:, :1]))
            if debug and c == WCH:
                op("scalar", "q_dbg", 16, lambda e, c=c: e.dma_start(
                    out=dbg_x, in_=xbuf[c % 6][:, :]))
                op("scalar", "q_dbg", 16, lambda e, c=c: e.dma_start(
                    out=dbg_xn, in_=xnbuf[c % 3][:, :]))

        c = t + 1
        if 6 <= c < NCH:
            wait("sync", "e_s", norm_done[c - 6])
            x_q[c] = load_x(c)

        c = t - 8
        if 0 <= c < NCH:
            oc = c - WCH
            if oc >= 0:
                wait("vector", "e_s", hoh_done[c])
                wait("vector", "e_v", chunkv_done.get(c - 2, 0))
                emit("vector", lambda e, c=c: e.tensor_tensor(
                    out=stage_xnl[(c - WCH) % 2][:, :], in0=pp[c % 3][:, :],
                    in1=stage_xnh[(c - WCH) % 2][:, :], op=ALU.subtract))
                selfsync("vector", "e_v")
                chunkv_done[c] = op("vector", "e_v", 1,
                                    lambda e, c=c, oc=oc: e.tensor_tensor(
                    out=stage_hol[oc % 2][:, :], in0=holo32[c % 2][:, :],
                    in1=stage_hoh[oc % 2][:, :], op=ALU.subtract))
                rel_done[c] = chunkv_done[c]

        if 0 <= c < NCH:
            oc = c - WCH
            if oc >= 0:
                wait("scalar", "e_v", chunkv_done[c])
                for scr_v, stg in ((xnh_v, stage_xnh), (xnl_v, stage_xnl),
                                   (hoh_v, stage_hoh), (hol_v, stage_hol)):
                    spx_cnt[oc] = op("scalar", f"q_spx{oc % 2}", 16,
                                     lambda e, oc=oc, scr_v=scr_v, stg=stg: e.dma_start(
                        out=scr_v[:, :, oc * CH:(oc + 1) * CH],
                        in_=stg[oc % 2].rearrange("p (k t) -> p k t", k=KT)))

        c = t - 5
        if 0 <= c < NCH:
            wait("tensor", "e_s", norm_done[c])
            wait("tensor", "q_c", consts_q)
            if c - 3 >= 0:
                assert (c - 3) in rel_done, f"rel_done missing for {c - 3}"
                if c - 3 >= WCH:
                    assert (c - 3) in xnh_done
            wait("tensor", "e_s", xnh_done.get(c - 3, 0))
            wait("tensor", "e_v", rel_done.get(c - 3, 0))
            p0, p1 = 2 * (c % 3), 2 * (c % 3) + 1
            for k in range(KT):
                bank, sl = (p0, k) if k < 4 else (p1, k - 4)
                fn = lambda e, c=c, k=k, bank=bank, sl=sl: e.transpose(
                    out=pt[bank][:, sl * 128:(sl + 1) * 128],
                    in_=xnbuf[c % 3][:, k * 128:(k + 1) * 128], identity=identf)
                if k == KT - 1:
                    tp_done[c] = op("tensor", "e_t", 1, fn)
                else:
                    emit("tensor", fn)

            if c >= BUILD_START and next_bu < n_build:
                n_units = (n_build - next_bu) if c == NCH - 1 else 2
                for _ in range(n_units):
                    if next_bu >= n_build:
                        break
                    u = next_bu
                    mi, nj = bu_list[u]
                    emit_build_splits(u - 2)
                    wait("tensor", "q_bld", build_in_q)
                    wait("tensor", "e_v", bu_split_done.get(u - 2, 0))
                    bank = 6 + u % 2
                    for k in range(KT):
                        fn = lambda e, k=k, mi=mi, nj=nj, bank=bank: e.matmul(
                            out=pt[bank][:, 0:512],
                            lhsT=convb[:, k, mi * 128:(mi + 1) * 128],
                            rhs=corrtb[:, k, nj * 512:(nj + 1) * 512],
                            start=(k == 0), stop=(k == KT - 1))
                        if k == KT - 1:
                            bu_mm_done[u] = op("tensor", "e_t", 1, fn)
                        else:
                            emit("tensor", fn)
                    pending_splits.append((u, mi, nj, bank))
                    next_bu += 1

        c = t - 6
        if 0 <= c < NCH:
            wait("vector", "e_t", tp_done[c])
            wait("vector", "e_s", hoh_done.get(c - 2, 0))
            wait("vector", "e_v", scan_done.get(c - 1, 0))
            wait("vector", "e_v", chunkv_done.get(c - 2, 0))
            if c == 0:
                wait("vector", "e_v", e_dcol)
            p0, p1 = 2 * (c % 3), 2 * (c % 3) + 1
            for k in range(KT):
                bank, sl = (p0, k) if k < 4 else (p1, k - 4)
                init = 0.0 if c == 0 else holo32[(c - 1) % 2][:, k * CH + CH - 1:
                                                              k * CH + CH]
                fn = lambda e, c=c, k=k, bank=bank, sl=sl, init=init: (
                    e.tensor_tensor_scan(
                        out=holo32[c % 2][:, k * CH:(k + 1) * CH], data0=dcol_bc,
                        data1=pt[bank][:, sl * 128:(sl + 1) * 128], initial=init,
                        op0=ALU.mult, op1=ALU.add))
                if k == KT - 1:
                    scan_done[c] = op("vector", "e_v", 1, fn)
                else:
                    emit("vector", fn)
            if c < WCH:
                rel_done[c] = scan_done[c]

        c = t - 7
        if 0 <= c < NCH:
            oc = c - WCH
            if oc >= 0:
                wait("scalar", "e_t", tp_done[c])
                wait("scalar", "e_v", scan_done[c])
                wait("scalar", f"q_spx{oc % 2}", spx_cnt.get(oc - 2, 0))
                selfsync("scalar", "e_s")
                xnh_done[c] = op("scalar", "e_s", 1,
                                 lambda e, c=c, oc=oc: e.copy(
                    out=stage_xnh[oc % 2][:, :], in_=pp[c % 3][:, :]))
                hoh_done[c] = op("scalar", "e_s", 1,
                                 lambda e, c=c, oc=oc: e.copy(
                    out=stage_hoh[oc % 2][:, :], in_=holo32[c % 2][:, :]))
            else:
                hoh_done[c] = 0

    emit_build_splits()
    assert next_bu == n_build
    P1_ev, P1_es, P1_et = C["e_v"], C["e_s"], C["e_t"]
    P1_spx = (C["q_spx0"], C["q_spx1"])

    wait("gpsimd", "e_v", P1_ev)
    wait("gpsimd", "e_s", P1_es)
    wait("gpsimd", "e_t", P1_et)
    wait("gpsimd", "q_spx0", P1_spx[0])
    wait("gpsimd", "q_spx1", P1_spx[1])
    for k in range(KT):
        op("gpsimd", "q_wgu", 16, lambda e, k=k: e.dma_start(
            out=wub[:, k, :], in_=wuT[k * 128:(k + 1) * 128, :]))

    wait("gpsimd", "e_t", bu_mm_done[n_build - 1])
    for k in range(KT):
        op("gpsimd", "q_wgu", 16, lambda e, k=k: e.dma_start(
            out=wgb[:, k, :], in_=wgT[k * 128:(k + 1) * 128, :]))
    W_gu = C["q_wgu"]

    wait("tensor", "e_v", bu_split_done[n_build - 1])
    wait("vector", "e_t", P1_et)
    wait("scalar", "e_v", P1_ev)

    sev_done = {}
    in2_q = {}
    sps_cnt = {}
    u = 0
    for tg in range(NTG):
        if p2_slab_next[0] <= tg:
            emit_p2_slab(tg)
            p2_slab_next[0] = tg + 1
        for n in range(NT):
            wait("sync", "e_v", sev_done.get(u - 2, 0))
            op("sync", f"q_in2_{u % 2}", 16, lambda e, u=u, n=n, tg=tg: e.dma_start(
                out=xn2h[u % 2][:, 0:512], in_=xnh_v[:, n, tg * TGS:(tg + 1) * TGS]))
            in2_q[u] = op("sync", f"q_in2_{u % 2}", 16, lambda e, u=u, n=n, tg=tg: e.dma_start(
                out=xn2l[u % 2][:, 0:512], in_=xnl_v[:, n, tg * TGS:(tg + 1) * TGS]))

            bank = 6 + u % 2
            wait("tensor", "e_v", sev_done.get(u - 2, 0))
            wait("tensor", f"q_ho2_{tg % 2}", ho2_q[tg])
            nmm = 0
            for (lh, rh) in ((mth, ho2h), (mth, ho2l), (mtl, ho2h)):
                for k in range(KT):
                    nmm += 1
                    fn = lambda e, lh=lh, rh=rh, k=k, n=n, tg=tg, bank=bank, \
                        st=(nmm == 1), sp=(nmm == 24): e.matmul(
                        out=pt[bank][:, 0:512], lhsT=lh[:, k, n * 128:(n + 1) * 128],
                        rhs=rh[tg % 2][:, k, :], start=st, stop=sp)
                    if nmm == 24:
                        ap_done[u] = op("tensor", "e_t", 1, fn)
                    else:
                        emit("tensor", fn)

            wait("vector", "e_t", ap_done[u])
            wait("vector", f"q_in2_{u % 2}", in2_q[u])
            wait("vector", f"q_sps{u % 2}", sps_cnt.get(u - 2, 0))
            wait("vector", "e_v", sev_done.get(u - 2, 0))
            op("vector", "e_v", 1, lambda e, u=u, bank=bank: e.tensor_tensor(
                out=s_stage[u % 2][:, 0:512], in0=pt[bank][:, 0:512],
                in1=xn2h[u % 2][:, 0:512], op=ALU.add))
            selfsync("vector", "e_v")
            sev_done[u] = op("vector", "e_v", 1, lambda e, u=u: e.tensor_tensor(
                out=s_stage[u % 2][:, 0:512], in0=s_stage[u % 2][:, 0:512],
                in1=xn2l[u % 2][:, 0:512], op=ALU.add))

            wait("scalar", "e_v", sev_done[u])
            sps_cnt[u] = op("scalar", f"q_sps{u % 2}", 16,
                            lambda e, u=u, n=n, tg=tg: e.dma_start(
                out=s_scr[n * 128:(n + 1) * 128, tg * TGS:(tg + 1) * TGS],
                in_=s_stage[u % 2][:, 0:512]))
            u += 1

    P2_ev, P2_es, P2_et = C["e_v"], C["e_s"], C["e_t"]
    P2_sps = (C["q_sps0"], C["q_sps1"])

    wait("gpsimd", "e_v", P2_ev)
    wait("gpsimd", "e_t", P2_et)
    for k in range(16):
        op("gpsimd", "q_wd", 16, lambda e, k=k: e.dma_start(
            out=wdb_a[:, k, :], in_=wdT[k * 128:(k + 1) * 128, :]))
    W_da = C["q_wd"]

    wait("tensor", "e_v", P2_ev)
    wait("tensor", "q_wgu", W_gu)
    wait("vector", "e_t", P2_et)
    wait("scalar", "e_v", P2_ev)
    wait("sync", "e_v", P2_ev)

    in3a_q = {}
    gate_done = {}
    up_done = {}
    sg_done = {}
    h32_done = {}
    hl_done = {}
    sph_cnt = {}
    u = 0
    for tg in range(NTG):
        if tg >= 2:
            wait("sync", "e_t", up_done.get((tg - 1) * HT - 1, 0))
        op("sync", f"q_in3a{tg % 2}", 16, lambda e, tg=tg: e.dma_start(
            out=xn3h[tg % 2][:, :, :], in_=xnh_v[:, :, tg * TGS:(tg + 1) * TGS]))
        in3a_q[tg] = op("sync", f"q_in3a{tg % 2}", 16, lambda e, tg=tg: e.dma_start(
            out=xn3l[tg % 2][:, :, :], in_=xnl_v[:, :, tg * TGS:(tg + 1) * TGS]))

        for m in range(HT):
            bA = u % 2
            bB = 2 + u % 2
            wait("tensor", f"q_in3a{tg % 2}", in3a_q[tg])
            wait("tensor", "e_s", sg_done.get(u - 2, 0))
            nmm = 0
            for src in (xn3h, xn3l):
                for k in range(KT):
                    nmm += 1
                    fn = lambda e, src=src, k=k, m=m, tg=tg, bA=bA, \
                        st=(nmm == 1), sp=(nmm == 16): e.matmul(
                        out=pt[bA][:, 0:512], lhsT=wgb[:, k, m * 128:(m + 1) * 128],
                        rhs=src[tg % 2][:, k, :], start=st, stop=sp)
                    if nmm == 16:
                        gate_done[u] = op("tensor", "e_t", 1, fn)
                    else:
                        emit("tensor", fn)
            wait("tensor", "e_v", h32_done.get(u - 2, 0))
            nmm = 0
            for src in (xn3h, xn3l):
                for k in range(KT):
                    nmm += 1
                    fn = lambda e, src=src, k=k, m=m, tg=tg, bB=bB, \
                        st=(nmm == 1), sp=(nmm == 16): e.matmul(
                        out=pt[bB][:, 0:512], lhsT=wub[:, k, m * 128:(m + 1) * 128],
                        rhs=src[tg % 2][:, k, :], start=st, stop=sp)
                    if nmm == 16:
                        up_done[u] = op("tensor", "e_t", 1, fn)
                    else:
                        emit("tensor", fn)

            wait("scalar", "e_t", gate_done[u])
            wait("scalar", "e_v", h32_done.get(u - 2, 0))
            sg_done[u] = op("scalar", "e_s", 1, lambda e, u=u, bA=bA: e.activation(
                out=sg_st[u % 2][:, 0:512], in_=pt[bA][:, 0:512], func=AF.Silu,
                scale=1.0 / 127.0))
            wait("vector", "e_t", up_done[u])
            wait("vector", "e_s", sg_done[u])
            h32_done[u] = op("vector", "e_v", 1, lambda e, u=u, bB=bB: e.scalar_tensor_tensor(
                out=h32_st[u % 2][:, 0:512], in0=pt[bB][:, 0:512], scalar=1.0 / 127.0,
                in1=sg_st[u % 2][:, 0:512], op0=ALU.mult, op1=ALU.mult))
            wait("scalar", "e_v", h32_done[u])
            wait("scalar", f"q_sph{u % 2}", sph_cnt.get(u - 2, 0))
            hh_done = op("scalar", "e_s", 1, lambda e, u=u: e.copy(
                out=hh_st[u % 2][:, 0:512], in_=h32_st[u % 2][:, 0:512]))
            wait("vector", "e_s", hh_done)
            hl_done[u] = op("vector", "e_v", 1, lambda e, u=u: e.tensor_tensor(
                out=hl_st[u % 2][:, 0:512], in0=h32_st[u % 2][:, 0:512],
                in1=hh_st[u % 2][:, 0:512], op=ALU.subtract))
            wait("scalar", "e_v", hl_done[u])
            op("scalar", f"q_sph{u % 2}", 16, lambda e, u=u, m=m, tg=tg: e.dma_start(
                out=hh_scr[m * 128:(m + 1) * 128, tg * TGS:(tg + 1) * TGS],
                in_=hh_st[u % 2][:, 0:512]))
            sph_cnt[u] = op("scalar", f"q_sph{u % 2}", 16,
                            lambda e, u=u, m=m, tg=tg: e.dma_start(
                out=hl_scr[m * 128:(m + 1) * 128, tg * TGS:(tg + 1) * TGS],
                in_=hl_st[u % 2][:, 0:512]))
            u += 1

    P3a_ev, P3a_es, P3a_et = C["e_v"], C["e_s"], C["e_t"]
    P3a_sph = (C["q_sph0"], C["q_sph1"])

    wait("gpsimd", "e_t", P3a_et)
    wait("gpsimd", "e_v", P3a_ev)
    for k in range(16, HT):
        op("gpsimd", "q_wd", 16, lambda e, k=k: e.dma_start(
            out=wdb_b[:, k - 16, :], in_=wdT[k * 128:(k + 1) * 128, :]))
    W_d = C["q_wd"]

    wait("tensor", "e_v", P3a_ev)
    wait("tensor", "q_wd", W_d)
    wait("vector", "e_t", P3a_et)
    wait("scalar", "e_v", P3a_ev)
    wait("sync", "e_t", P3a_et)
    wait("gpsimd", "e_t", P3a_et)
    wait("gpsimd", "e_v", P3a_ev)
    wait("sync", "q_sph0", P3a_sph[0])
    wait("sync", "q_sph1", P3a_sph[1])
    wait("sync", "q_sps0", P2_sps[0])
    wait("sync", "q_sps1", P2_sps[1])

    in3b_q = {}
    sx_q = {}
    dn_done = {}
    g_done = {}
    out_done = {}
    out_cnt = {}
    u = 0
    for tg in range(NTG):
        if tg >= 2:
            wait("sync", "e_t", dn_done.get((tg - 1) * NT - 1, 0))
        op("sync", f"q_h3{tg % 2}", 16, lambda e, tg=tg: e.dma_start(
            out=hh3[tg % 2][:, :, :], in_=hh_v[:, :, tg * TGS:(tg + 1) * TGS]))
        in3b_q[tg] = op("sync", f"q_h3{tg % 2}", 16, lambda e, tg=tg: e.dma_start(
            out=hl3[tg % 2][:, :, :], in_=hl_v[:, :, tg * TGS:(tg + 1) * TGS]))

        for n in range(NT):
            bank = 4 + u % 4
            wait("gpsimd", "e_v", out_done.get(u - 2, 0))
            op("gpsimd", f"q_sx{u % 2}", 16, lambda e, u=u, n=n, tg=tg: e.dma_start(
                out=s_in[u % 2][:, 0:512],
                in_=s_scr[n * 128:(n + 1) * 128, tg * TGS:(tg + 1) * TGS]))
            sx_q[u] = op("gpsimd", f"q_sx{u % 2}", 16, lambda e, u=u, n=n, tg=tg: e.dma_start(
                out=x_in[u % 2][:, 0:512],
                in_=xT[n * 128:(n + 1) * 128, tg * TGS:(tg + 1) * TGS]))

            wait("tensor", f"q_h3{tg % 2}", in3b_q[tg])
            wait("tensor", "e_s", g_done.get(u - 4, 0))
            nmm = 0
            for src in (hh3, hl3):
                for k in range(HT):
                    nmm += 1
                    wd_l = (wdb_a[:, k, n * 128:(n + 1) * 128] if k < 16
                            else wdb_b[:, k - 16, n * 128:(n + 1) * 128])
                    fn = lambda e, src=src, k=k, tg=tg, bank=bank, wd_l=wd_l, \
                        st=(nmm == 1), sp=(nmm == 48): e.matmul(
                        out=pt[bank][:, 0:512], lhsT=wd_l,
                        rhs=src[tg % 2][:, k, :], start=st, stop=sp)
                    if nmm == 48:
                        dn_done[u] = op("tensor", "e_t", 1, fn)
                    else:
                        emit("tensor", fn)

            wait("scalar", "e_t", dn_done[u])
            wait("scalar", "e_v", out_done.get(u - 2, 0))
            g_done[u] = op("scalar", "e_s", 1, lambda e, u=u, bank=bank: e.activation(
                out=g_st[u % 2][:, 0:512], in_=pt[bank][:, 0:512], func=AF.Silu,
                scale=1.0 / 127.0))
            wait("vector", "e_s", g_done[u])
            wait("vector", f"q_sx{u % 2}", sx_q[u])
            wait("vector", f"q_out{u % 2}", out_cnt.get(u - 2, 0))
            wait("vector", "e_v", out_done.get(u - 2, 0))
            op("vector", "e_v", 1, lambda e, u=u: e.tensor_tensor(
                out=out_st[u % 2][:, 0:512], in0=g_st[u % 2][:, 0:512],
                in1=s_in[u % 2][:, 0:512], op=ALU.mult))
            selfsync("vector", "e_v")
            out_done[u] = op("vector", "e_v", 1, lambda e, u=u: e.tensor_tensor(
                out=out_st[u % 2][:, 0:512], in0=out_st[u % 2][:, 0:512],
                in1=x_in[u % 2][:, 0:512], op=ALU.add))
            wait("scalar", "e_v", out_done[u])
            out_cnt[u] = op("scalar", f"q_out{u % 2}", 16,
                            lambda e, u=u, n=n, tg=tg: e.dma_start(
                out=outT[n * 128:(n + 1) * 128, tg * TGS:(tg + 1) * TGS],
                in_=out_st[u % 2][:, 0:512]))
            u += 1

    wait("scalar", "q_out0", C["q_out0"])
    wait("scalar", "q_out1", C["q_out1"])

    with nc.Block() as block:
        @block.sync
        def _(eng):
            for fn in OPS["sync"]:
                fn(eng)

        @block.vector
        def _(eng):
            for fn in OPS["vector"]:
                fn(eng)

        @block.scalar
        def _(eng):
            for fn in OPS["scalar"]:
                fn(eng)

        @block.tensor
        def _(eng):
            for fn in OPS["tensor"]:
                fn(eng)

        @block.gpsimd
        def _(eng):
            for fn in OPS["gpsimd"]:
                fn(eng)

    es.close()
    return nc


def kernel(L0_attn_x, L0_attn_bind_key, L0_attn_unbind_key, L0_attn_gate,
           L0_attn_decay, L0_ffn_w_gate, L0_ffn_w_up, L0_ffn_w_down):
    OWN, W_ = 2048, 512
    x = np.ascontiguousarray(np.asarray(L0_attn_x, dtype=np.float32))
    kb = np.asarray(L0_attn_bind_key, dtype=np.float32)
    ku = np.asarray(L0_attn_unbind_key, dtype=np.float32)
    gate = np.asarray(L0_attn_gate, dtype=np.float32).reshape(-1)[0]
    decay = np.asarray(L0_attn_decay, dtype=np.float32).reshape(())

    k_ = np.arange(D)[:, None]
    i_ = np.arange(D)[None, :]
    conv = np.ascontiguousarray(kb[(k_ - i_) % D])
    corrt = np.ascontiguousarray(ku[(k_ - i_) % D])
    wgT = np.ascontiguousarray(np.asarray(L0_ffn_w_gate, dtype=np.int32).T)
    wuT = np.ascontiguousarray(np.asarray(L0_ffn_w_up, dtype=np.int32).T)
    wdT = np.ascontiguousarray(np.asarray(L0_ffn_w_down, dtype=np.int32).T)
    gcol = np.full((128, 1), gate, np.float32)
    dcol = np.full((128, 1), decay, np.float32)
    identf = np.eye(128, dtype=np.float32)

    nc = build_nc(OWN, W_)
    in_maps = []
    for c in range(8):
        b, q = c // 4, c % 4
        t0 = q * OWN
        xw = np.zeros((W_ + OWN, D), np.float32)
        if q > 0:
            xw[:W_] = x[b, t0 - W_:t0]
        xw[W_:] = x[b, t0:t0 + OWN]
        in_maps.append(dict(
            xw=xw, xT=np.ascontiguousarray(x[b, t0:t0 + OWN].T),
            conv=conv, corrt=corrt, wgT=wgT, wuT=wuT, wdT=wdT,
            gcol=gcol, dcol=dcol, identf=identf))
    res = run_bass_kernel_spmd(nc, in_maps, core_ids=list(range(8)))
    out = np.empty_like(x)
    for c in range(8):
        b, q = c // 4, c % 4
        out[b, q * OWN:(q + 1) * OWN] = res.results[c]["outT"].T
    return out
